# version 06
# baseline (speedup 1.0000x reference)
import numpy as np
import ml_dtypes
from contextlib import ExitStack

import concourse.bass as bass
import concourse.mybir as mybir
from concourse.bass_utils import run_bass_kernel_spmd

F32 = mybir.dt.float32
BF16 = mybir.dt.bfloat16
AF = mybir.ActivationFunctionType
ALU = mybir.AluOpType
AX = mybir.AxisListType

D = 1024
T = 2048
TT = T // 128
HC = D // 128
E = 8
FF = 4096
TH = T // 2
THT = TH // 128
NF_A = 1.0 / 32.0
SC_B = 1.0 / 32.0
LN_EPS = 1e-5
NEG = -1.0e30

FCH = 1024
NFC = FF // FCH
NCH = E * NFC


class SemC:

    def __init__(self, stack, nc, name):
        self.h = stack.enter_context(nc.semaphore(name))
        self.base = 0
        self.emitted = 0

    def inc(self, ins, n=1):
        self.emitted += n
        return ins.then_inc(self.h, n)

    def advance(self, total):
        assert self.emitted == total, (self.h, self.emitted, total)
        self.base += total
        self.emitted = 0

    def v(self, k):
        return self.base + k


class ChainEdge:

    def __init__(self, stack, nc, name):
        self.h = stack.enter_context(nc.semaphore(name))
        self.n = 0

    def cut(self, ins):
        self.n += 1
        ins.then_inc(self.h, 1)
        return ins

    def wait(self, eng):
        eng.wait_ge(self.h, self.n)


def build_program(n_sub=4):
    nc = bass.Bass("TRN2", target_bir_lowering=False)
    st = ExitStack()

    xin = nc.dram_tensor("xin", [T, D], F32, kind="ExternalInput")
    out = nc.dram_tensor("out", [T, D], F32, kind="ExternalOutput")
    sp0 = nc.dram_tensor("sp0", [T, D], F32, kind="Internal")
    sp1 = nc.dram_tensor("sp1", [T, D], F32, kind="Internal")
    aWqT = nc.dram_tensor("aWqT", [128, HC * D], F32, kind="ExternalInput")
    aWkT = nc.dram_tensor("aWkT", [128, HC * D], F32, kind="ExternalInput")
    aWvT = nc.dram_tensor("aWvT", [128, HC * D], F32, kind="ExternalInput")
    bWq = nc.dram_tensor("bWq", [128, HC * D], F32, kind="ExternalInput")
    bWk = nc.dram_tensor("bWk", [128, HC * D], F32, kind="ExternalInput")
    bWv = nc.dram_tensor("bWv", [128, HC * D], F32, kind="ExternalInput")
    bqt = nc.dram_tensor("bqt", [128, HC], F32, kind="ExternalInput")
    bkt = nc.dram_tensor("bkt", [128, HC], F32, kind="ExternalInput")
    bvr = nc.dram_tensor("bvr", [1, D], F32, kind="ExternalInput")
    gwT = nc.dram_tensor("gwT", [D, E], F32, kind="ExternalInput")
    gch = nc.dram_tensor("gch", [E, NFC, 128, HC * FCH], F32, kind="ExternalInput")
    uch = nc.dram_tensor("uch", [E, NFC, 128, HC * FCH], F32, kind="ExternalInput")
    dch = nc.dram_tensor("dch", [E, NFC, 128, HC * D], F32, kind="ExternalInput")

    ident_b = nc.inline_tensor(np.eye(128, dtype=ml_dtypes.bfloat16), name="identb")
    ident_f = nc.inline_tensor(np.eye(128, dtype=np.float32), name="identf")
    cam_np = np.where(np.arange(128)[:, None] >= np.arange(128)[None, :],
                      0.0, NEG).astype(np.float32)
    cam_c = nc.inline_tensor(cam_np, name="camc")
    sel_np = np.zeros((8, 8 * 128), dtype=ml_dtypes.bfloat16)
    for e in range(8):
        sel_np[e, e * 128:(e + 1) * 128] = 1.0
    sel_c = nc.inline_tensor(sel_np, name="selc")
    ones_c = nc.inline_tensor(np.ones((1, 128), dtype=np.float32), name="onesc")

    A = nc.alloc_sbuf_tensor
    ids_b = A("ids_b", [128, 128], BF16)
    ids_f = A("ids_f", [128, 128], F32)
    cam_s = A("cam_s", [128, 128], F32)
    sel_s = A("sel_s", [8, 8 * 128], BF16)
    ones_s = A("ones_s", [1, 128], F32)
    bq_s = A("bq_s", [128, HC], F32)
    bk32 = A("bk32", [128, HC], F32)
    bvB = A("bvB", [128, D], F32)
    gw_s = A("gw_s", [128, HC * E], BF16)
    xt = A("xt", [128, 2 * D], F32)
    xt2 = A("xt2", [128, 2 * D], F32)
    xnew = A("xnew", [128, 2 * D], F32)
    xlt = A("xlt", [128, 2 * D], BF16)
    lsum = A("lsum", [128, 2], F32)
    lsq = A("lsq", [128, 2], F32)
    lmean = A("lmean", [128, 2], F32)
    lmsq = A("lmsq", [128, 2], F32)
    lm2 = A("lm2", [128, 2], F32)
    lvar = A("lvar", [128, 2], F32)
    llog = A("llog", [128, 2], F32)
    lrstd = A("lrstd", [128, 2], F32)
    lnegmb = A("lnegmb", [128, 2], F32)
    negm = A("negm", [128, 2], F32)
    rsum = A("rsum", [128, 2], F32)
    winv = A("winv", [128, 2], F32)
    stmx = A("stmx", [128, TT], F32)
    strs = A("strs", [128, TT], F32)
    gmx = A("gmx", [128, 1], F32)
    fts = A("fts", [128, TT], F32)
    gw8 = A("gw8", [128, TT], F32)
    gsm = A("gsm", [128, 1], F32)
    m11 = A("m11", [1, 1], F32)
    negM = A("negM", [128, 1], F32)
    sinv = A("sinv", [128, 1], F32)
    rmx = A("rmx", [128, 2], F32)
    re8 = A("re8", [128, 8], F32)
    rq1 = A("rq1", [128, 8], F32)
    rq2 = A("rq2", [128, 8], F32)
    rm1 = A("rm1", [128, 2], F32)
    rm2 = A("rm2", [128, 2], F32)
    rden = A("rden", [128, 2], F32)
    rw1 = A("rw1", [128, 2], F32)
    rw2 = A("rw2", [128, 2], F32)
    rcmb = A("rcmb", [128, 8], F32)
    rc16 = A("rc16", [128, 8], BF16)
    combT8 = A("combT8", [8, TH], BF16)

    sem_names = [
        "sx0", "sx1", "sx2_0", "sx2_1", "st0", "st1", "swa", "swf0", "swf1",
        "s_sq", "v_st", "s_rs", "v_nm", "s_norm", "pe_t", "v_te", "s_te",
        "pe_r", "v_rm", "s_rexp", "v_se", "v_cmb", "s_cb", "pe_t8", "v_t8",
        "pe_q", "s_qe", "v_qe", "pe_bc", "v_bc",
        "pe_s", "v_max", "s_exp", "v_misc", "s_et", "pe_a", "v_res", "s_au",
        "pe_cb", "v_cb", "pe_g", "pe_u", "s_act", "v_hact", "pe_d", "v_ev",
        "v_f3", "v_h1",
    ]
    S = {n: SemC(st, nc, n) for n in sem_names}
    vch = ChainEdge(st, nc, "vch")
    sch = ChainEdge(st, nc, "sch")
    wcnt = {"n": 0}

    with nc.Block("init") as blk:
        @blk.sync
        def _(sy):
            for dst, src in ((ids_b, ident_b), (ids_f, ident_f), (cam_s, cam_c),
                             (sel_s, sel_c), (ones_s, ones_c), (bq_s, bqt),
                             (bk32, bkt)):
                S["sx0"].inc(sy.dma_start(dst[:, :], src.ap()), 16)
            S["sx0"].inc(sy.dma_start(bvB[0:1, :], bvr.ap()), 16)
            sy.wait_ge(S["sx0"].h, S["sx0"].v(16 * 8))

        @blk.gpsimd
        def _(g):
            src = gwT.ap().rearrange("(hc p) e -> p hc e", p=128)
            dst = gw_s[:, :].rearrange("p (hc e) -> p hc e", hc=HC)
            S["swa"].inc(g.dma_start(dst, src), 16)
            g.wait_ge(S["swa"].h, S["swa"].v(16))
    S["sx0"].advance(8 * 16)
    S["swa"].advance(16)

    with nc.Block("bk32b") as blk:
        @blk.vector
        def _(v):
            v.tensor_scalar_mul(bk32[:, :], bk32[:, :], SC_B)

    def emit_ln(tag, xsrc, ntiles, goff, xlnT, Tlen, psT, ext=None):
        s_sq, v_st, s_rs, v_nm, s_norm, pe_t, v_te, s_te = (
            S["s_sq"], S["v_st"], S["s_rs"], S["v_nm"], S["s_norm"],
            S["pe_t"], S["v_te"], S["s_te"])
        sx = [S["sx0"], S["sx1"]]
        ext = ext or {}

        def hook(eng, name, i=None):
            fn = ext.get(name)
            if fn is not None:
                fn(eng) if i is None else fn(eng, i)

        with nc.Block(f"ln{tag}") as blk:
            @blk.sync
            def _(sy):
                sy.wait_ge(S["st0"].h, S["st0"].v(0))
                sy.wait_ge(S["st1"].h, S["st1"].v(0))
                for i in range(ntiles):
                    if i >= 2:
                        sy.wait_ge(s_norm.h, s_norm.v(i - 1))
                    sx[i % 2].inc(sy.dma_start(
                        xt[:, (i % 2) * D:(i % 2) * D + D],
                        xsrc.ap()[goff + i * 128:goff + (i + 1) * 128, :]), 16)
                hook(sy, "tail_sync")

            if "tail_gpsimd" in ext:
                @blk.gpsimd
                def _(g):
                    hook(g, "tail_gpsimd")

            @blk.vector
            def _(v):
                for i in range(ntiles):
                    b = i % 2
                    v.wait_ge(sx[b].h, sx[b].v(16 * (i // 2 + 1)))
                    vch.cut(v.reduce_sum(lsum[:, b:b + 1], xt[:, b * D:b * D + D],
                                         axis=AX.X))
                    v.wait_ge(s_sq.h, s_sq.v(i + 1))
                    vch.wait(v)
                    v.tensor_scalar_mul(lmean[:, b:b + 1], lsum[:, b:b + 1], 1.0 / D)
                    vch.cut(v.tensor_scalar_mul(lmsq[:, b:b + 1], lsq[:, b:b + 1],
                                                1.0 / D))
                    vch.wait(v)
                    vch.cut(v.tensor_mul(lm2[:, b:b + 1], lmean[:, b:b + 1],
                                         lmean[:, b:b + 1]))
                    vch.wait(v)
                    vch.cut(v.tensor_sub(lvar[:, b:b + 1], lmsq[:, b:b + 1],
                                         lm2[:, b:b + 1]))
                    vch.wait(v)
                    v_st.inc(v.tensor_scalar_add(lvar[:, b:b + 1],
                                                 lvar[:, b:b + 1], LN_EPS))
                    v.wait_ge(s_rs.h, s_rs.v(i + 1))
                    v_nm.inc(v.scalar_tensor_tensor(
                        lnegmb[:, b:b + 1], lmean[:, b:b + 1], -1.0,
                        lrstd[:, b:b + 1], ALU.mult, ALU.mult))
                    for hc in range(0, HC, 2):
                        k = i * HC + hc
                        v.wait_ge(pe_t.h, pe_t.v(k + 1))
                        v_te.inc(v.tensor_copy(
                            xlnT[:, hc * Tlen + i * 128:hc * Tlen + (i + 1) * 128],
                            psT[hc % 2][:, :]))
                    hook(v, "tile_vector", i)
                hook(v, "tail_vector")

            @blk.scalar
            def _(s):
                for i in range(ntiles):
                    b = i % 2
                    s.wait_ge(sx[b].h, sx[b].v(16 * (i // 2 + 1)))
                    s_sq.inc(s.activation(xnew[:, 0:D], xt[:, b * D:b * D + D],
                                          AF.Square, accum_out=lsq[:, b:b + 1]))
                    s.wait_ge(v_st.h, v_st.v(i + 1))
                    sch.cut(s.activation(llog[:, b:b + 1], lvar[:, b:b + 1], AF.Ln))
                    sch.wait(s)
                    s_rs.inc(s.activation(lrstd[:, b:b + 1], llog[:, b:b + 1],
                                          AF.Exp, scale=-0.5))
                    s.wait_ge(s_rs.h, s_rs.v(i + 1))
                    s.wait_ge(v_nm.h, v_nm.v(i + 1))
                    s_norm.inc(s.activation(xlt[:, b * D:b * D + D],
                                            xt[:, b * D:b * D + D], AF.Identity,
                                            scale=lrstd[:, b:b + 1],
                                            bias=lnegmb[:, b:b + 1]))
                    for hc in range(1, HC, 2):
                        k = i * HC + hc
                        s.wait_ge(pe_t.h, pe_t.v(k + 1))
                        s_te.inc(s.copy(
                            xlnT[:, hc * Tlen + i * 128:hc * Tlen + (i + 1) * 128],
                            psT[hc % 2][:, :]))
                    hook(s, "tile_scalar", i)
                hook(s, "tail_scalar")

            @blk.tensor
            def _(t):
                for i in range(ntiles):
                    b = i % 2
                    t.wait_ge(s_norm.h, s_norm.v(i + 1))
                    for hc in range(HC):
                        k = i * HC + hc
                        if k >= 2:
                            pk = k - 2
                            if pk % 2 == 0:
                                t.wait_ge(v_te.h, v_te.v(pk // 2 + 1))
                            else:
                                t.wait_ge(s_te.h, s_te.v(pk // 2 + 1))
                        pe_t.inc(t.transpose(
                            psT[hc % 2][:, :],
                            xlt[:, b * D + hc * 128:b * D + (hc + 1) * 128],
                            ids_b[:, :]))
                    hook(t, "tile_tensor", i)
                hook(t, "tail_tensor")

        sx[0].advance(16 * ((ntiles + 1) // 2))
        sx[1].advance(16 * (ntiles // 2))
        s_sq.advance(ntiles)
        v_st.advance(ntiles)
        s_rs.advance(ntiles)
        v_nm.advance(ntiles)
        s_norm.advance(ntiles)
        pe_t.advance(ntiles * HC)
        v_te.advance(ntiles * HC // 2)
        s_te.advance(ntiles * HC // 2)

    def emit_attn(kind, li, xsrc, xdst):
        isb = kind == "b"
        sub = ExitStack()
        subR = ExitStack()
        qT = sub.enter_context(nc.sbuf_tensor(f"qT{li}", [128, HC * T], BF16))
        kT = sub.enter_context(nc.sbuf_tensor(f"kT{li}", [128, HC * T], BF16))
        v_sb = sub.enter_context(nc.sbuf_tensor(f"v{li}", [128, TT * D], BF16))
        xlnT = subR.enter_context(
            nc.sbuf_tensor(f"xlnTa{li}", [128, HC * T], BF16, side="right"))
        W_ab = [subR.enter_context(
            nc.sbuf_tensor(f"Wab{li}_{i}", [128, HC * D], BF16, side="right"))
            for i in range(2)]

        wq, wk, wv = (aWqT, aWkT, aWvT) if kind == "a" else (bWq, bWk, bWv)
        sw, pe_q, s_qe, v_qe, pe_bc, v_bc = (S["swa"], S["pe_q"], S["s_qe"],
                                             S["v_qe"], S["pe_bc"], S["v_bc"])

        def wdma(g, wt, dst):
            sw.inc(g.dma_start(dst[:, :], wt.ap()), 16)

        with nc.Block(f"aw{li}") as blk:
            @blk.gpsimd
            def _(g):
                wdma(g, wq, W_ab[0])
                wdma(g, wk, W_ab[1])

        psc = ExitStack()
        psT = [psc.enter_context(nc.psum_tensor(f"psTa{li}{i}", [128, 128], BF16))
               for i in range(2)]
        pqk = [psc.enter_context(nc.psum_tensor(f"pqk{li}{i}", [128, 512], F32))
               for i in range(2)]
        pbc = psc.enter_context(nc.psum_tensor(f"pbc{li}", [128, D], F32))

        jobs = ([("q", dt, tc) for tc in range(4) for dt in range(HC)]
                + [("k", dt, tc) for tc in range(4) for dt in range(HC)]
                + [("v", s_, dc) for s_ in range(TT) for dc in range(2)])
        NQ = HC * 4

        def emit_job(t, jn):
            kj, a, b = jobs[jn]
            pb = jn % 2
            if jn == 0:
                t.wait_ge(sw.h, sw.v(32))
            if jn == 2 * NQ:
                t.wait_ge(sw.h, sw.v(48))
            if jn >= 2:
                pj = jn - 2
                if jobs[pj][0] == "v":
                    t.wait_ge(v_qe.h, v_qe.v(pj - 2 * NQ + 1))
                else:
                    t.wait_ge(s_qe.h, s_qe.v(pj + 1))
            need = (4 * b + 4 if kj in ("q", "k") else a + 1) * 4
            t.wait_ge(S["v_te"].h, S["v_te"].v(need))
            t.wait_ge(S["s_te"].h, S["s_te"].v(need))
            for hc in range(HC):
                if kj == "q":
                    ins = t.matmul(pqk[pb][:, :],
                                   W_ab[0][:, hc * D + a * 128:hc * D + (a + 1) * 128],
                                   xlnT[:, hc * T + b * 512:hc * T + (b + 1) * 512],
                                   start=(hc == 0), stop=(hc == HC - 1))
                elif kj == "k":
                    ins = t.matmul(pqk[pb][:, :],
                                   W_ab[1][:, hc * D + a * 128:hc * D + (a + 1) * 128],
                                   xlnT[:, hc * T + b * 512:hc * T + (b + 1) * 512],
                                   start=(hc == 0), stop=(hc == HC - 1))
                else:
                    ins = t.matmul(pqk[pb][:, :],
                                   xlnT[:, hc * T + a * 128:hc * T + (a + 1) * 128],
                                   W_ab[0][:, hc * D + b * 512:hc * D + (b + 1) * 512],
                                   start=(hc == 0), stop=(hc == HC - 1))
            pe_q.inc(ins)

        def emit_evac(eng, jn):
            kj, a, b = jobs[jn]
            pb = jn % 2
            eng.wait_ge(pe_q.h, pe_q.v(jn + 1))
            if kj == "v":
                col = a * D + b * 512
                if isb:
                    v_qe.inc(eng.tensor_add(v_sb[:, col:col + 512],
                                            pqk[pb][:, :],
                                            bvB[:, b * 512:b * 512 + 512]))
                else:
                    v_qe.inc(eng.tensor_scalar_mul(v_sb[:, col:col + 512],
                                                   pqk[pb][:, :], NF_A))
                return
            dst = qT if kj == "q" else kT
            col = a * T + b * 512
            if isb:
                bias = (bq_s if kj == "q" else bk32)[:, a:a + 1]
                scale = 1.0 if kj == "q" else SC_B
                s_qe.inc(eng.activation(dst[:, col:col + 512], pqk[pb][:, :],
                                        AF.Identity, bias=bias, scale=scale))
            else:
                s_qe.inc(eng.copy(dst[:, col:col + 512], pqk[pb][:, :]))

        st_pe = {"n": 0}
        st_sc = {"n": 0}

        def tile_tensor(t, i):
            if isb and i == 0:
                for dc in range(2):
                    pe_bc.inc(t.matmul(pbc[:, dc * 512:dc * 512 + 512],
                                       ones_s[0:1, :],
                                       bvB[0:1, dc * 512:dc * 512 + 512],
                                       start=True, stop=True))
            if i >= 3:
                hi = min(2 * (i - 2), NQ)
                while st_pe["n"] < hi:
                    emit_job(t, st_pe["n"])
                    st_pe["n"] += 1

        def tail_tensor(t):
            for jn in range(st_pe["n"], len(jobs)):
                emit_job(t, jn)

        def tile_scalar(s, i):
            if i >= 4:
                hi = min(2 * (i - 3), NQ)
                while st_sc["n"] < hi:
                    emit_evac(s, st_sc["n"])
                    st_sc["n"] += 1

        def tail_scalar(s):
            for jn in range(st_sc["n"], 2 * NQ):
                emit_evac(s, jn)

        def tail_vector(v):
            if isb:
                for dc in range(2):
                    v.wait_ge(pe_bc.h, pe_bc.v(dc + 1))
                    vch.cut(v.tensor_copy(
                        bvB[:, dc * 512:dc * 512 + 512],
                        pbc[:, dc * 512:dc * 512 + 512]))
                vch.wait(v)
            for jn in range(2 * NQ, len(jobs)):
                emit_evac(v, jn)

        def tail_gpsimd(g):
            g.wait_ge(pe_q.h, pe_q.v(NQ))
            wdma(g, wv, W_ab[0])

        emit_ln(f"a{li}", xsrc, TT, 0, xlnT, T, psT,
                ext={"tile_tensor": tile_tensor, "tail_tensor": tail_tensor,
                     "tile_scalar": tile_scalar, "tail_scalar": tail_scalar,
                     "tail_vector": tail_vector, "tail_gpsimd": tail_gpsimd})

        pe_q.advance(len(jobs))
        s_qe.advance(2 * NQ)
        v_qe.advance(TT * 2)
        sw.advance(48)
        wcnt["n"] += 1
        if isb:
            pe_bc.advance(2)
        psc.close()
        subR.close()

        e_sb = sub.enter_context(nc.sbuf_tensor(f"e{li}", [128, T], BF16))
        eT = sub.enter_context(nc.sbuf_tensor(f"eT{li}", [128, T], BF16))
        au = (sub.enter_context(nc.sbuf_tensor(f"au{li}", [128, TT * D], BF16))
              if isb else None)

        pe_s, v_max, s_exp, v_misc, s_et, pe_a, v_res, s_au, pe_t = (
            S["pe_s"], S["v_max"], S["s_exp"], S["v_misc"], S["s_et"],
            S["pe_a"], S["v_res"], S["s_au"], S["pe_t"])
        sx2 = [S["sx2_0"], S["sx2_1"]]
        sst = [S["st0"], S["st1"]]

        def n_sb(i):
            return 4 if not isb else (((i + 1) * 128 + 511) // 512)

        def n_tr(i):
            return TT if not isb else (i + 1)

        psc = ExitStack()
        Sps = psc.enter_context(nc.psum_tensor(f"Sps{li}", [128, T], F32))
        att = psc.enter_context(nc.psum_tensor(f"attp{li}", [128, D], F32))
        psT2 = [psc.enter_context(nc.psum_tensor(f"psTt{li}{i}", [128, 128], BF16))
                for i in range(2)]

        def emit_scores(t, i):
            for sb in range(n_sb(i)):
                for dt in range(HC):
                    ins = t.matmul(Sps[:, sb * 512:(sb + 1) * 512],
                                   qT[:, dt * T + i * 128:dt * T + (i + 1) * 128],
                                   kT[:, dt * T + sb * 512:dt * T + (sb + 1) * 512],
                                   start=(dt == 0), stop=(dt == HC - 1))
            return ins

        if isb:

            pgm = att[0:1, 0:128]
            pbc1 = att[:, 512:513]
            with nc.Block(f"bp2{li}") as blk:
                def emit_attb(t, i, tr0):
                    if i >= 1:
                        t.wait_ge(s_au.h, s_au.v(i))
                    for sc in range(n_tr(i)):
                        t.wait_ge(s_et.h, s_et.v(tr0 + sc + 1))
                        for dc in range(2):
                            ins = t.matmul(att[:, dc * 512:(dc + 1) * 512],
                                           eT[:, sc * 128:(sc + 1) * 128],
                                           v_sb[:, sc * D + dc * 512:sc * D + (dc + 1) * 512],
                                           start=(sc == 0), stop=(sc == n_tr(i) - 1))
                    pe_a.inc(ins)

                @blk.tensor
                def _(t):
                    trc = 0
                    for i in range(TT):
                        if i >= 1:
                            t.wait_ge(s_exp.h, s_exp.v(i))
                        pe_s.inc(emit_scores(t, i))
                        if i >= 1:
                            emit_attb(t, i - 1, trc - n_tr(i - 1))
                        t.wait_ge(s_exp.h, s_exp.v(i + 1))
                        for sc in range(n_tr(i)):
                            if trc >= 2:
                                t.wait_ge(s_et.h, s_et.v(trc - 1))
                            pe_t.inc(t.transpose(psT2[trc % 2][:, :],
                                                 e_sb[:, sc * 128:(sc + 1) * 128],
                                                 ids_b[:, :]))
                            trc += 1
                    emit_attb(t, TT - 1, trc - n_tr(TT - 1))

                @blk.vector
                def _(v):
                    for i in range(TT):
                        b = i % 2
                        v.wait_ge(pe_s.h, pe_s.v(i + 1))
                        vch.cut(v.tensor_add(Sps[:, i * 128:(i + 1) * 128],
                                             Sps[:, i * 128:(i + 1) * 128],
                                             cam_s[:, :]))
                        vch.wait(v)
                        vch.cut(v.reduce_max(stmx[:, i:i + 1],
                                             Sps[:, 0:(i + 1) * 128], axis=AX.X))
                        vch.wait(v)
                        if i >= 2:
                            v.wait_ge(s_exp.h, s_exp.v(i - 1))
                        v_max.inc(v.tensor_scalar_mul(negm[:, b:b + 1],
                                                      stmx[:, i:i + 1], -1.0))

                @blk.scalar
                def _(s):
                    trc = 0
                    for i in range(TT):
                        b = i % 2
                        w = (i + 1) * 128
                        s.wait_ge(v_max.h, v_max.v(i + 1))
                        s_exp.inc(s.activation(e_sb[:, 0:w], Sps[:, 0:w], AF.Exp,
                                               bias=negm[:, b:b + 1],
                                               accum_out=strs[:, i:i + 1]))
                        for sc in range(n_tr(i)):
                            s.wait_ge(pe_t.h, pe_t.v(trc + 1))
                            s_et.inc(s.copy(eT[:, sc * 128:(sc + 1) * 128],
                                            psT2[trc % 2][:, :]))
                            trc += 1
                        if i >= 1:
                            s.wait_ge(pe_a.h, pe_a.v(i))
                            s_au.inc(s.copy(au[:, (i - 1) * D:i * D], att[:, :]))
                    s.wait_ge(pe_a.h, pe_a.v(TT))
                    s_au.inc(s.copy(au[:, (TT - 1) * D:TT * D], att[:, :]))

            ntr_tot = sum(n_tr(i) for i in range(TT))
            pe_s.advance(TT)
            v_max.advance(TT)
            s_exp.advance(TT)
            pe_t.advance(ntr_tot)
            s_et.advance(ntr_tot)
            pe_a.advance(TT)
            s_au.advance(TT)

            with nc.Block(f"gm1{li}") as blk:
                @blk.vector
                def _(v):
                    v.reduce_max(gmx[:, :], stmx[:, :], axis=AX.X)
            with nc.Block(f"gm2{li}") as blk:
                @blk.tensor
                def _(t):
                    t.transpose(pgm, gmx[:, :], ids_f[:, :])
            with nc.Block(f"gm3{li}") as blk:
                @blk.vector
                def _(v):
                    v.reduce_max(m11[:, :], pgm, axis=AX.X)
            with nc.Block(f"gm4{li}") as blk:
                @blk.tensor
                def _(t):
                    t.matmul(pbc1, ones_s[0:1, :], m11[0:1, :],
                             start=True, stop=True)
            with nc.Block(f"gm5{li}") as blk:
                @blk.vector
                def _(v):
                    v.tensor_scalar_mul(negM[:, :], pbc1, -1.0)
            with nc.Block(f"sm0{li}") as blk:
                @blk.scalar
                def _(s):
                    s.activation(fts[:, :], stmx[:, :], AF.Exp, bias=negM[:, 0:1])
            with nc.Block(f"sm1{li}") as blk:
                @blk.vector
                def _(v):
                    vch.cut(v.tensor_mul(gw8[:, :], fts[:, :], strs[:, :]))
                    vch.wait(v)
                    v.reduce_sum(gsm[:, :], gw8[:, :], axis=AX.X)
            with nc.Block(f"sm2{li}") as blk:
                @blk.tensor
                def _(t):
                    t.transpose(pgm, gsm[:, :], ids_f[:, :])
            with nc.Block(f"sm3{li}") as blk:
                @blk.vector
                def _(v):
                    v.reduce_sum(m11[:, :], pgm, axis=AX.X)
            with nc.Block(f"sm4{li}") as blk:
                @blk.tensor
                def _(t):
                    t.matmul(pbc1, ones_s[0:1, :], m11[0:1, :],
                             start=True, stop=True)
            with nc.Block(f"sm5{li}") as blk:
                @blk.vector
                def _(v):
                    vch.cut(v.reciprocal(sinv[:, :], pbc1))
                    vch.wait(v)
                    v.tensor_scalar(fts[:, :], fts[:, :], sinv[:, 0:1], None,
                                    ALU.mult)

            with nc.Block(f"bres{li}") as blk:
                @blk.sync
                def _(sy):
                    for i in range(TT + 2):
                        if i >= 2:
                            sy.wait_ge(v_res.h, v_res.v(i - 1))
                            b2 = (i - 2) % 2
                            sst[b2].inc(sy.dma_start(
                                xdst.ap()[(i - 2) * 128:(i - 1) * 128, :],
                                xnew[:, b2 * D:b2 * D + D]), 16)
                        if i < TT:
                            b = i % 2
                            sx2[b].inc(sy.dma_start(
                                xt2[:, b * D:b * D + D],
                                xsrc.ap()[i * 128:(i + 1) * 128, :]), 16)

                @blk.vector
                def _(v):
                    for i in range(TT):
                        b = i % 2
                        v.wait_ge(sx2[b].h, sx2[b].v(16 * (i // 2 + 1)))
                        if i >= 2:
                            v.wait_ge(sst[b].h, sst[b].v(16 * (i // 2)))
                        v_res.inc(v.scalar_tensor_tensor(
                            xnew[:, b * D:b * D + D], au[:, i * D:(i + 1) * D],
                            fts[:, i:i + 1], xt2[:, b * D:b * D + D],
                            ALU.mult, ALU.add))
            sx2[0].advance(16 * (TT // 2))
            sx2[1].advance(16 * (TT // 2))
            sst[0].advance(16 * (TT // 2))
            sst[1].advance(16 * (TT // 2))
            v_res.advance(TT)

        else:
            with nc.Block(f"acore{li}") as blk:
                def emit_att(t, i, tr0):
                    if i >= 1:
                        t.wait_ge(v_res.h, v_res.v(i))
                    for sc in range(TT):
                        t.wait_ge(s_et.h, s_et.v(tr0 + sc + 1))
                        for dc in range(2):
                            ins = t.matmul(att[:, dc * 512:(dc + 1) * 512],
                                           eT[:, sc * 128:(sc + 1) * 128],
                                           v_sb[:, sc * D + dc * 512:sc * D + (dc + 1) * 512],
                                           start=(sc == 0), stop=(sc == TT - 1))
                    pe_a.inc(ins)

                @blk.tensor
                def _(t):
                    trc = 0
                    for i in range(TT):
                        if i >= 1:
                            t.wait_ge(s_exp.h, s_exp.v(i))
                        pe_s.inc(emit_scores(t, i))
                        if i >= 1:
                            emit_att(t, i - 1, trc - TT)
                        t.wait_ge(s_exp.h, s_exp.v(i + 1))
                        for sc in range(TT):
                            if trc >= 2:
                                t.wait_ge(s_et.h, s_et.v(trc - 1))
                            pe_t.inc(t.transpose(psT2[trc % 2][:, :],
                                                 e_sb[:, sc * 128:(sc + 1) * 128],
                                                 ids_b[:, :]))
                            trc += 1
                    emit_att(t, TT - 1, trc - TT)

                def emit_stt(v, i):
                    b = i % 2
                    v.wait_ge(pe_a.h, pe_a.v(i + 1))
                    v.wait_ge(sx2[b].h, sx2[b].v(16 * (i // 2 + 1)))
                    if i >= 2:
                        v.wait_ge(sst[b].h, sst[b].v(16 * (i // 2)))
                    v.wait_ge(v_misc.h, v_misc.v(i + 1))
                    v_res.inc(v.scalar_tensor_tensor(
                        xnew[:, b * D:b * D + D], att[:, :],
                        winv[:, b:b + 1], xt2[:, b * D:b * D + D],
                        ALU.mult, ALU.add))

                @blk.vector
                def _(v):
                    for i in range(TT):
                        b = i % 2
                        v.wait_ge(pe_s.h, pe_s.v(i + 1))
                        v_max.inc(v.reduce_max(negm[:, b:b + 1], Sps[:, :],
                                               axis=AX.X, negate=True))
                        v.wait_ge(s_exp.h, s_exp.v(i + 1))
                        v_misc.inc(v.reciprocal(winv[:, b:b + 1],
                                                rsum[:, b:b + 1]))
                        if i >= 1:
                            emit_stt(v, i - 1)
                    emit_stt(v, TT - 1)

                @blk.scalar
                def _(s):
                    trc = 0
                    for i in range(TT):
                        b = i % 2
                        s.wait_ge(v_max.h, v_max.v(i + 1))
                        s_exp.inc(s.activation(e_sb[:, :], Sps[:, :], AF.Exp,
                                               bias=negm[:, b:b + 1],
                                               accum_out=rsum[:, b:b + 1]))
                        for sc in range(TT):
                            s.wait_ge(pe_t.h, pe_t.v(trc + 1))
                            s_et.inc(s.copy(eT[:, sc * 128:(sc + 1) * 128],
                                            psT2[trc % 2][:, :]))
                            trc += 1

                @blk.sync
                def _(sy):
                    for i in range(TT + 2):
                        if i >= 2:
                            sy.wait_ge(v_res.h, v_res.v(i - 1))
                            b2 = (i - 2) % 2
                            sst[b2].inc(sy.dma_start(
                                xdst.ap()[(i - 2) * 128:(i - 1) * 128, :],
                                xnew[:, b2 * D:b2 * D + D]), 16)
                        if i < TT:
                            b = i % 2
                            sx2[b].inc(sy.dma_start(
                                xt2[:, b * D:b * D + D],
                                xsrc.ap()[i * 128:(i + 1) * 128, :]), 16)

            pe_s.advance(TT)
            v_max.advance(TT)
            s_exp.advance(TT)
            pe_t.advance(TT * TT)
            s_et.advance(TT * TT)
            pe_a.advance(TT)
            v_misc.advance(TT)
            v_res.advance(TT)
            sx2[0].advance(16 * (TT // 2))
            sx2[1].advance(16 * (TT // 2))
            sst[0].advance(16 * (TT // 2))
            sst[1].advance(16 * (TT // 2))

        psc.close()
        sub.close()

    def emit_ffn(li, xsrc, xdst):
        sub = ExitStack()
        Wg_sb = [sub.enter_context(nc.sbuf_tensor(f"Wg{li}{i}", [128, HC * FCH], BF16))
                 for i in range(2)]
        Wu_sb = [sub.enter_context(nc.sbuf_tensor(f"Wu{li}{i}", [128, HC * FCH], BF16))
                 for i in range(2)]
        Wd_sb = [sub.enter_context(nc.sbuf_tensor(f"Wd{li}{i}", [128, HC * D], BF16))
                 for i in range(2)]
        xlnT = sub.enter_context(nc.sbuf_tensor(f"xlnTf{li}", [128, HC * TH], BF16))
        acc = sub.enter_context(nc.sbuf_tensor(f"acc{li}", [128, HC * TH], F32))
        hact = [sub.enter_context(nc.sbuf_tensor(f"hact{li}{i}", [128, HC * 512], BF16))
                for i in range(2)]
        cbB = [sub.enter_context(nc.sbuf_tensor(f"cbB{li}{i}", [128, TH], BF16))
               for i in range(2)]
        gg = sub.enter_context(nc.sbuf_tensor(f"gg{li}", [128, 2 * 512], BF16))

        swf = [S["swf0"], S["swf1"]]

        def issue_chunk(g, c, cb):
            e, fc = divmod(c, NFC)
            sw = swf[c % 2]
            sw.inc(g.dma_start(Wg_sb[cb][:, :], gch.ap()[e][fc]), 16)
            sw.inc(g.dma_start(Wu_sb[cb][:, :], uch.ap()[e][fc]), 16)
            sw.inc(g.dma_start(Wd_sb[cb][:, :], dch.ap()[e][fc]), 16)

        for half in range(2):
            goff = half * TH
            hl = f"{li}h{half}"

            with nc.Block(f"fw{hl}") as blk:
                @blk.gpsimd
                def _(g):
                    issue_chunk(g, 0, 0)
                    issue_chunk(g, 1, 1)

            psc = ExitStack()
            psT = [psc.enter_context(nc.psum_tensor(f"psTf{hl}{i}", [128, 128], BF16))
                   for i in range(2)]
            emit_ln(f"f{hl}", xsrc, THT, goff, xlnT, TH, psT)

            pr = psc.enter_context(nc.psum_tensor(f"pr{hl}", [128, 8], F32))
            pT8 = psc.enter_context(nc.psum_tensor(f"pT8{hl}", [8, 128], BF16))
            pe_r, v_rm, s_rexp, v_se, v_cmb, s_cb, pe_t8, v_t8 = (
                S["pe_r"], S["v_rm"], S["s_rexp"], S["v_se"], S["v_cmb"],
                S["s_cb"], S["pe_t8"], S["v_t8"])

            with nc.Block(f"frt{hl}") as blk:
                @blk.tensor
                def _(t):
                    for i in range(THT):
                        if i >= 1:
                            t.wait_ge(s_rexp.h, s_rexp.v(i))
                        for hc in range(HC):
                            ins = t.matmul(pr[:, :],
                                           xlnT[:, hc * TH + i * 128:hc * TH + (i + 1) * 128],
                                           gw_s[:, hc * E:(hc + 1) * E],
                                           start=(hc == 0), stop=(hc == HC - 1))
                        pe_r.inc(ins)
                        if i >= 1:
                            t.wait_ge(s_cb.h, s_cb.v(i))
                            if i >= 2:
                                t.wait_ge(v_t8.h, v_t8.v(i - 1))
                            pe_t8.inc(t.transpose(pT8[:, :], rc16[:, :],
                                                  ids_b[:, :]))
                    t.wait_ge(s_cb.h, s_cb.v(THT))
                    t.wait_ge(v_t8.h, v_t8.v(THT - 1))
                    pe_t8.inc(t.transpose(pT8[:, :], rc16[:, :], ids_b[:, :]))

                @blk.vector
                def _(v):
                    for i in range(THT):
                        b = i % 2
                        v.wait_ge(pe_r.h, pe_r.v(i + 1))
                        if i >= 2:
                            v.wait_ge(s_rexp.h, s_rexp.v(i - 1))
                        v_rm.inc(v.reduce_max(rmx[:, b:b + 1], pr[:, :],
                                              axis=AX.X, negate=True))
                        v.wait_ge(s_rexp.h, s_rexp.v(i + 1))
                        vch.cut(v.reduce_max(rm1[:, b:b + 1], re8[:, :],
                                             axis=AX.X))
                        vch.wait(v)
                        vch.cut(v.tensor_scalar(rq1[:, :], re8[:, :],
                                                rm1[:, b:b + 1], None,
                                                ALU.is_equal))
                        vch.wait(v)
                        vch.cut(v.tensor_mul(rq2[:, :], re8[:, :], rq1[:, :]))
                        vch.wait(v)
                        vch.cut(v.tensor_sub(rq2[:, :], re8[:, :], rq2[:, :]))
                        vch.wait(v)
                        vch.cut(v.reduce_max(rm2[:, b:b + 1], rq2[:, :],
                                             axis=AX.X))
                        vch.wait(v)
                        vch.cut(v.tensor_scalar(rq2[:, :], rq2[:, :],
                                                rm2[:, b:b + 1], None,
                                                ALU.is_equal))
                        vch.cut(v.tensor_add(rden[:, b:b + 1], rm1[:, b:b + 1],
                                             rm2[:, b:b + 1]))
                        vch.wait(v)
                        vch.cut(v.reciprocal(rden[:, b:b + 1], rden[:, b:b + 1]))
                        vch.wait(v)
                        v.tensor_mul(rw1[:, b:b + 1], rm1[:, b:b + 1],
                                     rden[:, b:b + 1])
                        vch.cut(v.tensor_mul(rw2[:, b:b + 1], rm2[:, b:b + 1],
                                             rden[:, b:b + 1]))
                        vch.wait(v)
                        vch.cut(v.tensor_scalar(rcmb[:, :], rq1[:, :],
                                                rw1[:, b:b + 1], None, ALU.mult))
                        vch.wait(v)
                        v_cmb.inc(v.scalar_tensor_tensor(
                            rcmb[:, :], rq2[:, :], rw2[:, b:b + 1], rcmb[:, :],
                            ALU.mult, ALU.add))
                        v.wait_ge(pe_t8.h, pe_t8.v(i + 1))
                        v_t8.inc(v.tensor_copy(combT8[0:8, i * 128:(i + 1) * 128],
                                               pT8[0:8, :]))

                @blk.scalar
                def _(s):
                    for i in range(THT):
                        b = i % 2
                        s.wait_ge(v_rm.h, v_rm.v(i + 1))
                        s_rexp.inc(s.activation(re8[:, :], pr[:, :], AF.Exp,
                                                bias=rmx[:, b:b + 1]))
                        s.wait_ge(v_cmb.h, v_cmb.v(i + 1))
                        if i >= 1:
                            s.wait_ge(pe_t8.h, pe_t8.v(i))
                        s_cb.inc(s.copy(rc16[:, :], rcmb[:, :]))

            pe_r.advance(THT)
            v_rm.advance(THT)
            s_rexp.advance(THT)
            v_cmb.advance(THT)
            s_cb.advance(THT)
            pe_t8.advance(THT)
            v_t8.advance(THT)
            psc.close()

            psc = ExitStack()
            pg = [psc.enter_context(nc.psum_tensor(f"pg{hl}{i}", [128, 512], F32))
                  for i in range(2)]
            pu = [psc.enter_context(nc.psum_tensor(f"pu{hl}{i}", [128, 512], F32))
                  for i in range(2)]
            pd = [psc.enter_context(nc.psum_tensor(f"pd{hl}{i}", [128, 512], F32))
                  for i in range(2)]
            pcb = psc.enter_context(nc.psum_tensor(f"pcb{hl}", [128, TH], F32))

            pe_cb, v_cb, pe_g, pe_u, s_act, v_hact, pe_d, v_ev = (
                S["pe_cb"], S["v_cb"], S["pe_g"], S["pe_u"], S["s_act"],
                S["v_hact"], S["pe_d"], S["v_ev"])

            with nc.Block(f"f2{hl}") as blk:
                @blk.tensor
                def _(t):
                    for c in range(NCH):
                        e, fc = divmod(c, NFC)
                        if fc == 0:
                            if e >= 1:
                                t.wait_ge(v_cb.h, v_cb.v(e))
                            for tc in range(2):
                                ins = t.matmul(pcb[:, tc * 512:(tc + 1) * 512],
                                               sel_s[0:8, e * 128:(e + 1) * 128],
                                               combT8[0:8, tc * 512:(tc + 1) * 512],
                                               start=True, stop=True)
                            pe_cb.inc(ins)
                        t.wait_ge(swf[c % 2].h, swf[c % 2].v(48 * (c // 2 + 1)))
                        for tb in range(2):
                            for j in range(HC):
                                gj = (c * 2 + tb) * HC + j
                                if gj >= 2:
                                    t.wait_ge(s_act.h, s_act.v(gj - 1))
                                for hc in range(HC):
                                    ins = t.matmul(pg[j % 2][:, :],
                                                   Wg_sb[c % 2][:, hc * FCH + j * 128:hc * FCH + (j + 1) * 128],
                                                   xlnT[:, hc * TH + tb * 512:hc * TH + (tb + 1) * 512],
                                                   start=(hc == 0), stop=(hc == HC - 1))
                                pe_g.inc(ins)
                                if gj >= 2:
                                    t.wait_ge(S["v_h1"].h, S["v_h1"].v(gj - 1))
                                for hc in range(HC):
                                    ins = t.matmul(pu[j % 2][:, :],
                                                   Wu_sb[c % 2][:, hc * FCH + j * 128:hc * FCH + (j + 1) * 128],
                                                   xlnT[:, hc * TH + tb * 512:hc * TH + (tb + 1) * 512],
                                                   start=(hc == 0), stop=(hc == HC - 1))
                                pe_u.inc(ins)
                            for h2 in range(HC):
                                gdh = (c * 2 + tb) * HC + h2
                                if gdh >= 2:
                                    t.wait_ge(v_ev.h, v_ev.v(gdh - 1))
                                for j in range(HC):
                                    if h2 == 0:
                                        t.wait_ge(v_hact.h,
                                                  v_hact.v((c * 2 + tb) * HC + j + 1))
                                    ins = t.matmul(pd[h2 % 2][:, :],
                                                   Wd_sb[c % 2][:, j * D + h2 * 128:j * D + (h2 + 1) * 128],
                                                   hact[tb][:, j * 512:(j + 1) * 512],
                                                   start=(j == 0), stop=(j == HC - 1))
                                pe_d.inc(ins)

                @blk.scalar
                def _(s):
                    for c in range(NCH):
                        for tb in range(2):
                            for j in range(HC):
                                gj = (c * 2 + tb) * HC + j
                                s.wait_ge(pe_g.h, pe_g.v(gj + 1))
                                if gj >= 2:
                                    s.wait_ge(S["v_h1"].h, S["v_h1"].v(gj - 1))
                                s_act.inc(s.activation(
                                    gg[:, (j % 2) * 512:(j % 2) * 512 + 512],
                                    pg[j % 2][:, :], AF.Gelu))

                @blk.vector
                def _(v):
                    for c in range(NCH):
                        e, fc = divmod(c, NFC)
                        if fc == 0:
                            v.wait_ge(pe_cb.h, pe_cb.v(e + 1))
                            v_cb.inc(v.tensor_copy(cbB[e % 2][:, :], pcb[:, :]))
                        for tb in range(2):
                            for j in range(HC):
                                gj = (c * 2 + tb) * HC + j
                                v.wait_ge(s_act.h, s_act.v(gj + 1))
                                v.wait_ge(pe_u.h, pe_u.v(gj + 1))
                                if c >= 1:
                                    v.wait_ge(pe_d.h,
                                              pe_d.v(((c - 1) * 2 + tb + 1) * HC))
                                S["v_h1"].inc(v.tensor_mul(
                                    hact[tb][:, j * 512:(j + 1) * 512],
                                    pu[j % 2][:, :],
                                    gg[:, (j % 2) * 512:(j % 2) * 512 + 512]))
                                v.wait_ge(S["v_h1"].h, S["v_h1"].v(gj + 1))
                                v_hact.inc(v.tensor_mul(
                                    hact[tb][:, j * 512:(j + 1) * 512],
                                    hact[tb][:, j * 512:(j + 1) * 512],
                                    cbB[e % 2][:, tb * 512:(tb + 1) * 512]))
                            for h2 in range(HC):
                                gdh = (c * 2 + tb) * HC + h2
                                v.wait_ge(pe_d.h, pe_d.v(gdh + 1))
                                dst = acc[:, h2 * TH + tb * 512:h2 * TH + (tb + 1) * 512]
                                if c == 0:
                                    v_ev.inc(v.tensor_copy(dst, pd[h2 % 2][:, :]))
                                else:
                                    v_ev.inc(v.tensor_tensor(dst, pd[h2 % 2][:, :],
                                                             dst, ALU.add))

                @blk.gpsimd
                def _(g):
                    for c in range(2, NCH):
                        g.wait_ge(pe_d.h, pe_d.v((c - 1) * 2 * HC))
                        issue_chunk(g, c, c % 2)

            pe_cb.advance(E)
            v_cb.advance(E)
            pe_g.advance(NCH * 2 * HC)
            pe_u.advance(NCH * 2 * HC)
            s_act.advance(NCH * 2 * HC)
            S["v_h1"].advance(NCH * 2 * HC)
            v_hact.advance(NCH * 2 * HC)
            pe_d.advance(NCH * 2 * HC)
            v_ev.advance(NCH * 2 * HC)
            swf[0].advance(48 * (NCH // 2))
            swf[1].advance(48 * (NCH // 2))
            psc.close()

            psc = ExitStack()
            pF3 = [psc.enter_context(nc.psum_tensor(f"pF3{hl}{i}", [128, 128], F32))
                   for i in range(2)]
            pe_t, v_f3 = S["pe_t"], S["v_f3"]
            sx2 = [S["sx2_0"], S["sx2_1"]]
            sst = [S["st0"], S["st1"]]
            with nc.Block(f"f3{hl}") as blk:
                @blk.tensor
                def _(t):
                    trc = 0
                    for i in range(THT):
                        for j in range(HC):
                            if trc >= 2:
                                t.wait_ge(v_f3.h, v_f3.v(trc - 1))
                            pe_t.inc(t.transpose(
                                pF3[trc % 2][:, :],
                                acc[:, j * TH + i * 128:j * TH + (i + 1) * 128],
                                ids_f[:, :]))
                            trc += 1

                @blk.vector
                def _(v):
                    trc = 0
                    for i in range(THT):
                        b = i % 2
                        v.wait_ge(sx2[b].h, sx2[b].v(16 * (i // 2 + 1)))
                        if i >= 2:
                            v.wait_ge(sst[b].h, sst[b].v(16 * (i // 2)))
                        for j in range(HC):
                            v.wait_ge(pe_t.h, pe_t.v(trc + 1))
                            v_f3.inc(v.tensor_tensor(
                                xnew[:, b * D + j * 128:b * D + (j + 1) * 128],
                                pF3[trc % 2][:, :],
                                xt2[:, b * D + j * 128:b * D + (j + 1) * 128],
                                ALU.add))
                            trc += 1

                @blk.sync
                def _(sy):
                    for i in range(THT + 2):
                        if i >= 2:
                            sy.wait_ge(v_f3.h, v_f3.v(HC * (i - 1)))
                            b2 = (i - 2) % 2
                            sst[b2].inc(sy.dma_start(
                                xdst.ap()[goff + (i - 2) * 128:goff + (i - 1) * 128, :],
                                xnew[:, b2 * D:b2 * D + D]), 16)
                        if i < THT:
                            b = i % 2
                            sx2[b].inc(sy.dma_start(
                                xt2[:, b * D:b * D + D],
                                xsrc.ap()[goff + i * 128:goff + (i + 1) * 128, :]), 16)

            pe_t.advance(THT * HC)
            v_f3.advance(THT * HC)
            sx2[0].advance(16 * ((THT + 1) // 2))
            sx2[1].advance(16 * (THT // 2))
            sst[0].advance(16 * ((THT + 1) // 2))
            sst[1].advance(16 * (THT // 2))
            psc.close()

        sub.close()

    chain = [
        ("a", xin, out if n_sub == 1 else sp0),
        ("f", sp0, out if n_sub == 2 else sp1),
        ("b", sp1, out if n_sub == 3 else sp0),
        ("f", sp0, out),
    ][:n_sub]

    for li, (kind, src, dst) in enumerate(chain):
        if kind == "f":
            emit_ffn(li, src, dst)
        else:
            emit_attn(kind, li, src, dst)

    with nc.Block("fin") as blk:
        @blk.sync
        def _(sy):
            sy.wait_ge(S["st0"].h, S["st0"].v(0))
            sy.wait_ge(S["st1"].h, S["st1"].v(0))

    st.close()
    return nc


def _prep_inputs(kw):
    f32 = np.float32
    x = np.asarray(kw["L0_attn_a_x"], dtype=f32)
    c = np.ascontiguousarray
    def attn_w(w_hd):
        return c(w_hd.reshape(HC, 128, D).transpose(1, 0, 2).reshape(128, HC * D))

    gup = np.asarray(kw["L0_ffn_gate_up_w"], f32)
    dw = np.asarray(kw["L0_ffn_down_w"], f32)
    def chunked_gu(part):
        t = part.reshape(E, NFC, FCH, HC, 128)
        return c(t.transpose(0, 1, 4, 3, 2).reshape(E, NFC, 128, HC * FCH))
    dt_ = dw.reshape(E, D, NFC, HC, 128)
    dch_np = c(dt_.transpose(0, 2, 4, 3, 1).reshape(E, NFC, 128, HC * D))
    base = {
        "aWqT": attn_w(np.asarray(kw["L0_attn_a_Wq"], f32).T),
        "aWkT": attn_w(np.asarray(kw["L0_attn_a_Wk"], f32).T),
        "aWvT": attn_w(np.asarray(kw["L0_attn_a_Wv"], f32).T),
        "bWq": attn_w(np.asarray(kw["L0_attn_b_Wq"], f32)),
        "bWk": attn_w(np.asarray(kw["L0_attn_b_Wk"], f32)),
        "bWv": attn_w(np.asarray(kw["L0_attn_b_Wv"], f32)),
        "bqt": c(np.asarray(kw["L0_attn_b_bq"], f32).reshape(HC, 128).T),
        "bkt": c(np.asarray(kw["L0_attn_b_bk"], f32).reshape(HC, 128).T),
        "bvr": c(np.asarray(kw["L0_attn_b_bv"], f32).reshape(1, D)),
        "gwT": c(np.asarray(kw["L0_ffn_gate_w"], f32).T),
        "gch": chunked_gu(gup[:, :FF, :]),
        "uch": chunked_gu(gup[:, FF:, :]),
        "dch": dch_np,
    }
    in_maps = []
    for core in range(8):
        m = dict(base)
        m["xin"] = c(x[core])
        in_maps.append(m)
    return in_maps


def _run(kw, n_sub=4):
    in_maps = _prep_inputs(kw)
    nc = build_program(n_sub)
    res = run_bass_kernel_spmd(nc, in_maps, core_ids=list(range(8)))
    return np.stack([r["out"] for r in res.results], axis=0)


def kernel(**kw):
    return _run(kw, 4)


# version 07
# speedup vs baseline: 1.0684x; 1.0684x over previous
import numpy as np
import ml_dtypes
from contextlib import ExitStack

import concourse.bass as bass
import concourse.mybir as mybir
from concourse.bass_utils import run_bass_kernel_spmd

F32 = mybir.dt.float32
BF16 = mybir.dt.bfloat16
AF = mybir.ActivationFunctionType
ALU = mybir.AluOpType
AX = mybir.AxisListType

D = 1024
T = 2048
TT = T // 128
HC = D // 128
E = 8
FF = 4096
TH = T // 2
THT = TH // 128
NF_A = 1.0 / 32.0
SC_B = 1.0 / 32.0
LN_EPS = 1e-5
NEG = -1.0e30

FCH = 1024
NFC = FF // FCH
NCH = E * NFC


class SemC:

    def __init__(self, stack, nc, name):
        self.h = stack.enter_context(nc.semaphore(name))
        self.base = 0
        self.emitted = 0

    def inc(self, ins, n=1):
        self.emitted += n
        return ins.then_inc(self.h, n)

    def advance(self, total):
        assert self.emitted == total, (self.h, self.emitted, total)
        self.base += total
        self.emitted = 0

    def v(self, k):
        return self.base + k


class ChainEdge:

    def __init__(self, stack, nc, name):
        self.h = stack.enter_context(nc.semaphore(name))
        self.n = 0

    def cut(self, ins):
        self.n += 1
        ins.then_inc(self.h, 1)
        return ins

    def wait(self, eng):
        eng.wait_ge(self.h, self.n)


def build_program(n_sub=4):
    nc = bass.Bass("TRN2", target_bir_lowering=False)
    st = ExitStack()

    xin = nc.dram_tensor("xin", [T, D], F32, kind="ExternalInput")
    out = nc.dram_tensor("out", [T, D], F32, kind="ExternalOutput")
    sp0 = nc.dram_tensor("sp0", [T, D], F32, kind="Internal")
    sp1 = nc.dram_tensor("sp1", [T, D], F32, kind="Internal")
    aWqT = nc.dram_tensor("aWqT", [128, HC * D], F32, kind="ExternalInput")
    aWkT = nc.dram_tensor("aWkT", [128, HC * D], F32, kind="ExternalInput")
    aWvT = nc.dram_tensor("aWvT", [128, HC * D], F32, kind="ExternalInput")
    bWq = nc.dram_tensor("bWq", [128, HC * D], F32, kind="ExternalInput")
    bWk = nc.dram_tensor("bWk", [128, HC * D], F32, kind="ExternalInput")
    bWv = nc.dram_tensor("bWv", [128, HC * D], F32, kind="ExternalInput")
    bqt = nc.dram_tensor("bqt", [128, HC], F32, kind="ExternalInput")
    bkt = nc.dram_tensor("bkt", [128, HC], F32, kind="ExternalInput")
    bvr = nc.dram_tensor("bvr", [1, D], F32, kind="ExternalInput")
    gwT = nc.dram_tensor("gwT", [D, E], F32, kind="ExternalInput")
    gch = nc.dram_tensor("gch", [E, NFC, 128, HC * FCH], F32, kind="ExternalInput")
    uch = nc.dram_tensor("uch", [E, NFC, 128, HC * FCH], F32, kind="ExternalInput")
    dch = nc.dram_tensor("dch", [E, NFC, 128, HC * D], F32, kind="ExternalInput")

    ident_b = nc.inline_tensor(np.eye(128, dtype=ml_dtypes.bfloat16), name="identb")
    ident_f = nc.inline_tensor(np.eye(128, dtype=np.float32), name="identf")
    cam_np = np.where(np.arange(128)[:, None] >= np.arange(128)[None, :],
                      0.0, NEG).astype(np.float32)
    cam_c = nc.inline_tensor(cam_np, name="camc")
    sel_np = np.zeros((8, 8 * 128), dtype=ml_dtypes.bfloat16)
    for e in range(8):
        sel_np[e, e * 128:(e + 1) * 128] = 1.0
    sel_c = nc.inline_tensor(sel_np, name="selc")
    ones_c = nc.inline_tensor(np.ones((1, 128), dtype=np.float32), name="onesc")

    A = nc.alloc_sbuf_tensor
    ids_b = A("ids_b", [128, 128], BF16)
    ids_f = A("ids_f", [128, 128], F32)
    cam_s = A("cam_s", [128, 128], F32)
    sel_s = A("sel_s", [8, 8 * 128], BF16)
    ones_s = A("ones_s", [1, 128], F32)
    bq_s = A("bq_s", [128, HC], F32)
    bk32 = A("bk32", [128, HC], F32)
    bvB = A("bvB", [128, D], F32)
    gw_s = A("gw_s", [128, HC * E], BF16)
    xt = A("xt", [128, 2 * D], F32)
    xt2 = A("xt2", [128, 2 * D], F32)
    xnew = A("xnew", [128, 2 * D], F32)
    xlt = A("xlt", [128, 2 * D], BF16)
    lsum = A("lsum", [128, 2], F32)
    lsq = A("lsq", [128, 2], F32)
    lmean = A("lmean", [128, 2], F32)
    lmsq = A("lmsq", [128, 2], F32)
    lm2 = A("lm2", [128, 2], F32)
    lvar = A("lvar", [128, 2], F32)
    llog = A("llog", [128, 2], F32)
    lrstd = A("lrstd", [128, 2], F32)
    lnegmb = A("lnegmb", [128, 2], F32)
    negm = A("negm", [128, 2], F32)
    rsum = A("rsum", [128, 2], F32)
    winv = A("winv", [128, 2], F32)
    stmx = A("stmx", [128, TT], F32)
    strs = A("strs", [128, TT], F32)
    gmx = A("gmx", [128, 1], F32)
    fts = A("fts", [128, TT], F32)
    gw8 = A("gw8", [128, TT], F32)
    gsm = A("gsm", [128, 1], F32)
    m11 = A("m11", [1, 1], F32)
    negM = A("negM", [128, 1], F32)
    sinv = A("sinv", [128, 1], F32)
    rmx = A("rmx", [128, 2], F32)
    re8 = A("re8", [128, 8], F32)
    rq1 = A("rq1", [128, 2 * 8], F32)
    rq2 = A("rq2", [128, 2 * 8], F32)
    rm1 = A("rm1", [128, 2], F32)
    rm2 = A("rm2", [128, 2], F32)
    rden = A("rden", [128, 2], F32)
    rw1 = A("rw1", [128, 2], F32)
    rw2 = A("rw2", [128, 2], F32)
    rcmb = A("rcmb", [128, 8], F32)
    rc16 = A("rc16", [128, 2 * 8], BF16)
    combT8 = A("combT8", [8, TH], BF16)

    sem_names = [
        "sx0", "sx1", "sx2_0", "sx2_1", "st0", "st1", "swa", "swf0", "swf1",
        "s_sq", "v_st", "s_rs", "v_nm", "s_norm", "pe_t", "v_te", "s_te",
        "pe_r", "v_rm", "s_rexp", "v_se", "v_cmb", "s_cb", "pe_t8", "v_t8",
        "pe_q", "s_qe", "v_qe", "pe_bc", "v_bc",
        "pe_s", "v_max", "s_exp", "v_misc", "s_et", "pe_a", "v_res", "s_au",
        "pe_cb", "v_cb", "pe_g", "pe_u", "s_act", "v_hact", "pe_d", "v_ev",
        "v_f3", "v_h1",
    ]
    S = {n: SemC(st, nc, n) for n in sem_names}
    vch = ChainEdge(st, nc, "vch")
    sch = ChainEdge(st, nc, "sch")
    wcnt = {"n": 0}

    with nc.Block("init") as blk:
        @blk.sync
        def _(sy):
            for dst, src in ((ids_b, ident_b), (ids_f, ident_f), (cam_s, cam_c),
                             (sel_s, sel_c), (ones_s, ones_c), (bq_s, bqt),
                             (bk32, bkt)):
                S["sx0"].inc(sy.dma_start(dst[:, :], src.ap()), 16)
            S["sx0"].inc(sy.dma_start(bvB[0:1, :], bvr.ap()), 16)
            sy.wait_ge(S["sx0"].h, S["sx0"].v(16 * 8))

        @blk.gpsimd
        def _(g):
            src = gwT.ap().rearrange("(hc p) e -> p hc e", p=128)
            dst = gw_s[:, :].rearrange("p (hc e) -> p hc e", hc=HC)
            S["swa"].inc(g.dma_start(dst, src), 16)
            g.wait_ge(S["swa"].h, S["swa"].v(16))
    S["sx0"].advance(8 * 16)
    S["swa"].advance(16)

    with nc.Block("bk32b") as blk:
        @blk.vector
        def _(v):
            v.tensor_scalar_mul(bk32[:, :], bk32[:, :], SC_B)

    def emit_ln(tag, xsrc, ntiles, goff, xlnT, Tlen, psT, ext=None):
        s_sq, v_st, s_rs, v_nm, s_norm, pe_t, v_te, s_te = (
            S["s_sq"], S["v_st"], S["s_rs"], S["v_nm"], S["s_norm"],
            S["pe_t"], S["v_te"], S["s_te"])
        sx = [S["sx0"], S["sx1"]]
        ext = ext or {}

        def hook(eng, name, i=None):
            fn = ext.get(name)
            if fn is not None:
                fn(eng) if i is None else fn(eng, i)

        with nc.Block(f"ln{tag}") as blk:
            @blk.sync
            def _(sy):
                sy.wait_ge(S["st0"].h, S["st0"].v(0))
                sy.wait_ge(S["st1"].h, S["st1"].v(0))
                for i in range(ntiles):
                    if i >= 2:
                        sy.wait_ge(s_norm.h, s_norm.v(i - 1))
                    sx[i % 2].inc(sy.dma_start(
                        xt[:, (i % 2) * D:(i % 2) * D + D],
                        xsrc.ap()[goff + i * 128:goff + (i + 1) * 128, :]), 16)
                hook(sy, "tail_sync")

            if "tail_gpsimd" in ext:
                @blk.gpsimd
                def _(g):
                    hook(g, "tail_gpsimd")

            @blk.vector
            def _(v):
                for i in range(ntiles):
                    b = i % 2
                    v.wait_ge(sx[b].h, sx[b].v(16 * (i // 2 + 1)))
                    vch.cut(v.reduce_sum(lsum[:, b:b + 1], xt[:, b * D:b * D + D],
                                         axis=AX.X))
                    v.wait_ge(s_sq.h, s_sq.v(i + 1))
                    vch.wait(v)
                    v.tensor_scalar_mul(lmean[:, b:b + 1], lsum[:, b:b + 1], 1.0 / D)
                    vch.cut(v.tensor_scalar_mul(lmsq[:, b:b + 1], lsq[:, b:b + 1],
                                                1.0 / D))
                    vch.wait(v)
                    vch.cut(v.tensor_mul(lm2[:, b:b + 1], lmean[:, b:b + 1],
                                         lmean[:, b:b + 1]))
                    vch.wait(v)
                    vch.cut(v.tensor_sub(lvar[:, b:b + 1], lmsq[:, b:b + 1],
                                         lm2[:, b:b + 1]))
                    vch.wait(v)
                    v_st.inc(v.tensor_scalar_add(lvar[:, b:b + 1],
                                                 lvar[:, b:b + 1], LN_EPS))
                    v.wait_ge(s_rs.h, s_rs.v(i + 1))
                    v_nm.inc(v.scalar_tensor_tensor(
                        lnegmb[:, b:b + 1], lmean[:, b:b + 1], -1.0,
                        lrstd[:, b:b + 1], ALU.mult, ALU.mult))
                    for hc in range(0, HC, 2):
                        k = i * HC + hc
                        v.wait_ge(pe_t.h, pe_t.v(k + 1))
                        v_te.inc(v.tensor_copy(
                            xlnT[:, hc * Tlen + i * 128:hc * Tlen + (i + 1) * 128],
                            psT[hc % 2][:, :]))
                    hook(v, "tile_vector", i)
                hook(v, "tail_vector")

            @blk.scalar
            def _(s):
                for i in range(ntiles):
                    b = i % 2
                    s.wait_ge(sx[b].h, sx[b].v(16 * (i // 2 + 1)))
                    s_sq.inc(s.activation(xnew[:, 0:D], xt[:, b * D:b * D + D],
                                          AF.Square, accum_out=lsq[:, b:b + 1]))
                    s.wait_ge(v_st.h, v_st.v(i + 1))
                    sch.cut(s.activation(llog[:, b:b + 1], lvar[:, b:b + 1], AF.Ln))
                    sch.wait(s)
                    s_rs.inc(s.activation(lrstd[:, b:b + 1], llog[:, b:b + 1],
                                          AF.Exp, scale=-0.5))
                    s.wait_ge(s_rs.h, s_rs.v(i + 1))
                    s.wait_ge(v_nm.h, v_nm.v(i + 1))
                    s_norm.inc(s.activation(xlt[:, b * D:b * D + D],
                                            xt[:, b * D:b * D + D], AF.Identity,
                                            scale=lrstd[:, b:b + 1],
                                            bias=lnegmb[:, b:b + 1]))
                    for hc in range(1, HC, 2):
                        k = i * HC + hc
                        s.wait_ge(pe_t.h, pe_t.v(k + 1))
                        s_te.inc(s.copy(
                            xlnT[:, hc * Tlen + i * 128:hc * Tlen + (i + 1) * 128],
                            psT[hc % 2][:, :]))
                    hook(s, "tile_scalar", i)
                hook(s, "tail_scalar")

            @blk.tensor
            def _(t):
                for i in range(ntiles):
                    b = i % 2
                    t.wait_ge(s_norm.h, s_norm.v(i + 1))
                    for hc in range(HC):
                        k = i * HC + hc
                        if k >= 2:
                            pk = k - 2
                            if pk % 2 == 0:
                                t.wait_ge(v_te.h, v_te.v(pk // 2 + 1))
                            else:
                                t.wait_ge(s_te.h, s_te.v(pk // 2 + 1))
                        pe_t.inc(t.transpose(
                            psT[hc % 2][:, :],
                            xlt[:, b * D + hc * 128:b * D + (hc + 1) * 128],
                            ids_b[:, :]))
                    hook(t, "tile_tensor", i)
                hook(t, "tail_tensor")

        sx[0].advance(16 * ((ntiles + 1) // 2))
        sx[1].advance(16 * (ntiles // 2))
        s_sq.advance(ntiles)
        v_st.advance(ntiles)
        s_rs.advance(ntiles)
        v_nm.advance(ntiles)
        s_norm.advance(ntiles)
        pe_t.advance(ntiles * HC)
        v_te.advance(ntiles * HC // 2)
        s_te.advance(ntiles * HC // 2)

    def emit_attn(kind, li, xsrc, xdst):
        isb = kind == "b"
        sub = ExitStack()
        subR = ExitStack()
        qT = sub.enter_context(nc.sbuf_tensor(f"qT{li}", [128, HC * T], BF16))
        kT = sub.enter_context(nc.sbuf_tensor(f"kT{li}", [128, HC * T], BF16))
        v_sb = sub.enter_context(nc.sbuf_tensor(f"v{li}", [128, TT * D], BF16))
        xlnT = subR.enter_context(
            nc.sbuf_tensor(f"xlnTa{li}", [128, HC * T], BF16, side="right"))
        W_ab = [subR.enter_context(
            nc.sbuf_tensor(f"Wab{li}_{i}", [128, HC * D], BF16, side="right"))
            for i in range(2)]

        wq, wk, wv = (aWqT, aWkT, aWvT) if kind == "a" else (bWq, bWk, bWv)
        sw, pe_q, s_qe, v_qe, pe_bc, v_bc = (S["swa"], S["pe_q"], S["s_qe"],
                                             S["v_qe"], S["pe_bc"], S["v_bc"])

        def wdma(g, wt, dst):
            sw.inc(g.dma_start(dst[:, :], wt.ap()), 16)

        with nc.Block(f"aw{li}") as blk:
            @blk.gpsimd
            def _(g):
                wdma(g, wq, W_ab[0])
                wdma(g, wk, W_ab[1])

        psc = ExitStack()
        psT = [psc.enter_context(nc.psum_tensor(f"psTa{li}{i}", [128, 128], BF16))
               for i in range(2)]
        pqk = [psc.enter_context(nc.psum_tensor(f"pqk{li}{i}", [128, 512], F32))
               for i in range(2)]
        pbc = psc.enter_context(nc.psum_tensor(f"pbc{li}", [128, D], F32))

        jobs = ([("q", dt, tc) for tc in range(4) for dt in range(HC)]
                + [("k", dt, tc) for tc in range(4) for dt in range(HC)]
                + [("v", s_, dc) for s_ in range(TT) for dc in range(2)])
        NQ = HC * 4

        def emit_job(t, jn):
            kj, a, b = jobs[jn]
            pb = jn % 2
            if jn == 0:
                t.wait_ge(sw.h, sw.v(32))
            if jn == 2 * NQ:
                t.wait_ge(sw.h, sw.v(48))
            if jn >= 2:
                pj = jn - 2
                if jobs[pj][0] == "v":
                    t.wait_ge(v_qe.h, v_qe.v(pj - 2 * NQ + 1))
                else:
                    t.wait_ge(s_qe.h, s_qe.v(pj + 1))
            need = (4 * b + 4 if kj in ("q", "k") else a + 1) * 4
            t.wait_ge(S["v_te"].h, S["v_te"].v(need))
            t.wait_ge(S["s_te"].h, S["s_te"].v(need))
            for hc in range(HC):
                if kj == "q":
                    ins = t.matmul(pqk[pb][:, :],
                                   W_ab[0][:, hc * D + a * 128:hc * D + (a + 1) * 128],
                                   xlnT[:, hc * T + b * 512:hc * T + (b + 1) * 512],
                                   start=(hc == 0), stop=(hc == HC - 1))
                elif kj == "k":
                    ins = t.matmul(pqk[pb][:, :],
                                   W_ab[1][:, hc * D + a * 128:hc * D + (a + 1) * 128],
                                   xlnT[:, hc * T + b * 512:hc * T + (b + 1) * 512],
                                   start=(hc == 0), stop=(hc == HC - 1))
                else:
                    ins = t.matmul(pqk[pb][:, :],
                                   xlnT[:, hc * T + a * 128:hc * T + (a + 1) * 128],
                                   W_ab[0][:, hc * D + b * 512:hc * D + (b + 1) * 512],
                                   start=(hc == 0), stop=(hc == HC - 1))
            pe_q.inc(ins)

        def emit_evac(eng, jn):
            kj, a, b = jobs[jn]
            pb = jn % 2
            eng.wait_ge(pe_q.h, pe_q.v(jn + 1))
            if kj == "v":
                col = a * D + b * 512
                if isb:
                    v_qe.inc(eng.tensor_add(v_sb[:, col:col + 512],
                                            pqk[pb][:, :],
                                            bvB[:, b * 512:b * 512 + 512]))
                else:
                    v_qe.inc(eng.tensor_scalar_mul(v_sb[:, col:col + 512],
                                                   pqk[pb][:, :], NF_A))
                return
            dst = qT if kj == "q" else kT
            col = a * T + b * 512
            if isb:
                bias = (bq_s if kj == "q" else bk32)[:, a:a + 1]
                scale = 1.0 if kj == "q" else SC_B
                s_qe.inc(eng.activation(dst[:, col:col + 512], pqk[pb][:, :],
                                        AF.Identity, bias=bias, scale=scale))
            else:
                s_qe.inc(eng.copy(dst[:, col:col + 512], pqk[pb][:, :]))

        st_pe = {"n": 0}
        st_sc = {"n": 0}

        def tile_tensor(t, i):
            if isb and i == 0:
                for dc in range(2):
                    pe_bc.inc(t.matmul(pbc[:, dc * 512:dc * 512 + 512],
                                       ones_s[0:1, :],
                                       bvB[0:1, dc * 512:dc * 512 + 512],
                                       start=True, stop=True))
            if i >= 3:
                hi = min(2 * (i - 2), NQ)
                while st_pe["n"] < hi:
                    emit_job(t, st_pe["n"])
                    st_pe["n"] += 1

        def tail_tensor(t):
            for jn in range(st_pe["n"], len(jobs)):
                emit_job(t, jn)

        def tile_scalar(s, i):
            if i >= 4:
                hi = min(2 * (i - 3), NQ)
                while st_sc["n"] < hi:
                    emit_evac(s, st_sc["n"])
                    st_sc["n"] += 1

        def tail_scalar(s):
            for jn in range(st_sc["n"], 2 * NQ):
                emit_evac(s, jn)

        def tail_vector(v):
            if isb:
                for dc in range(2):
                    v.wait_ge(pe_bc.h, pe_bc.v(dc + 1))
                    vch.cut(v.tensor_copy(
                        bvB[:, dc * 512:dc * 512 + 512],
                        pbc[:, dc * 512:dc * 512 + 512]))
                vch.wait(v)
            for jn in range(2 * NQ, len(jobs)):
                emit_evac(v, jn)

        def tail_gpsimd(g):
            g.wait_ge(pe_q.h, pe_q.v(NQ))
            wdma(g, wv, W_ab[0])

        emit_ln(f"a{li}", xsrc, TT, 0, xlnT, T, psT,
                ext={"tile_tensor": tile_tensor, "tail_tensor": tail_tensor,
                     "tile_scalar": tile_scalar, "tail_scalar": tail_scalar,
                     "tail_vector": tail_vector, "tail_gpsimd": tail_gpsimd})

        pe_q.advance(len(jobs))
        s_qe.advance(2 * NQ)
        v_qe.advance(TT * 2)
        sw.advance(48)
        wcnt["n"] += 1
        if isb:
            pe_bc.advance(2)
        psc.close()
        subR.close()

        e_sb = sub.enter_context(nc.sbuf_tensor(f"e{li}", [128, T], BF16))
        eT = sub.enter_context(nc.sbuf_tensor(f"eT{li}", [128, T], BF16))
        au = (sub.enter_context(nc.sbuf_tensor(f"au{li}", [128, TT * D], BF16))
              if isb else None)

        pe_s, v_max, s_exp, v_misc, s_et, pe_a, v_res, s_au, pe_t = (
            S["pe_s"], S["v_max"], S["s_exp"], S["v_misc"], S["s_et"],
            S["pe_a"], S["v_res"], S["s_au"], S["pe_t"])
        sx2 = [S["sx2_0"], S["sx2_1"]]
        sst = [S["st0"], S["st1"]]

        def n_sb(i):
            return 4 if not isb else (((i + 1) * 128 + 511) // 512)

        def n_tr(i):
            return TT if not isb else (i + 1)

        psc = ExitStack()
        Sps = psc.enter_context(nc.psum_tensor(f"Sps{li}", [128, T], F32))
        att = psc.enter_context(nc.psum_tensor(f"attp{li}", [128, D], F32))
        psT2 = [psc.enter_context(nc.psum_tensor(f"psTt{li}{i}", [128, 128], BF16))
                for i in range(2)]

        def emit_scores(t, i):
            for sb in range(n_sb(i)):
                for dt in range(HC):
                    ins = t.matmul(Sps[:, sb * 512:(sb + 1) * 512],
                                   qT[:, dt * T + i * 128:dt * T + (i + 1) * 128],
                                   kT[:, dt * T + sb * 512:dt * T + (sb + 1) * 512],
                                   start=(dt == 0), stop=(dt == HC - 1))
            return ins

        if isb:

            pgm = att[0:1, 0:128]
            pbc1 = att[:, 512:513]
            with nc.Block(f"bp2{li}") as blk:
                def emit_attb(t, i, tr0):
                    if i >= 1:
                        t.wait_ge(s_au.h, s_au.v(i))
                    for sc in range(n_tr(i)):
                        t.wait_ge(s_et.h, s_et.v(tr0 + sc + 1))
                        for dc in range(2):
                            ins = t.matmul(att[:, dc * 512:(dc + 1) * 512],
                                           eT[:, sc * 128:(sc + 1) * 128],
                                           v_sb[:, sc * D + dc * 512:sc * D + (dc + 1) * 512],
                                           start=(sc == 0), stop=(sc == n_tr(i) - 1))
                    pe_a.inc(ins)

                @blk.tensor
                def _(t):
                    trc = 0
                    for i in range(TT):
                        if i >= 1:
                            t.wait_ge(s_exp.h, s_exp.v(i))
                        pe_s.inc(emit_scores(t, i))
                        if i >= 1:
                            emit_attb(t, i - 1, trc - n_tr(i - 1))
                        t.wait_ge(s_exp.h, s_exp.v(i + 1))
                        for sc in range(n_tr(i)):
                            if trc >= 2:
                                t.wait_ge(s_et.h, s_et.v(trc - 1))
                            pe_t.inc(t.transpose(psT2[trc % 2][:, :],
                                                 e_sb[:, sc * 128:(sc + 1) * 128],
                                                 ids_b[:, :]))
                            trc += 1
                    emit_attb(t, TT - 1, trc - n_tr(TT - 1))

                @blk.vector
                def _(v):
                    for i in range(TT):
                        b = i % 2
                        v.wait_ge(pe_s.h, pe_s.v(i + 1))
                        vch.cut(v.tensor_add(Sps[:, i * 128:(i + 1) * 128],
                                             Sps[:, i * 128:(i + 1) * 128],
                                             cam_s[:, :]))
                        vch.wait(v)
                        vch.cut(v.reduce_max(stmx[:, i:i + 1],
                                             Sps[:, 0:(i + 1) * 128], axis=AX.X))
                        vch.wait(v)
                        if i >= 2:
                            v.wait_ge(s_exp.h, s_exp.v(i - 1))
                        v_max.inc(v.tensor_scalar_mul(negm[:, b:b + 1],
                                                      stmx[:, i:i + 1], -1.0))

                @blk.scalar
                def _(s):
                    trc = 0
                    for i in range(TT):
                        b = i % 2
                        w = (i + 1) * 128
                        s.wait_ge(v_max.h, v_max.v(i + 1))
                        s_exp.inc(s.activation(e_sb[:, 0:w], Sps[:, 0:w], AF.Exp,
                                               bias=negm[:, b:b + 1],
                                               accum_out=strs[:, i:i + 1]))
                        for sc in range(n_tr(i)):
                            s.wait_ge(pe_t.h, pe_t.v(trc + 1))
                            s_et.inc(s.copy(eT[:, sc * 128:(sc + 1) * 128],
                                            psT2[trc % 2][:, :]))
                            trc += 1
                        if i >= 1:
                            s.wait_ge(pe_a.h, pe_a.v(i))
                            s_au.inc(s.copy(au[:, (i - 1) * D:i * D], att[:, :]))
                    s.wait_ge(pe_a.h, pe_a.v(TT))
                    s_au.inc(s.copy(au[:, (TT - 1) * D:TT * D], att[:, :]))

            ntr_tot = sum(n_tr(i) for i in range(TT))
            pe_s.advance(TT)
            v_max.advance(TT)
            s_exp.advance(TT)
            pe_t.advance(ntr_tot)
            s_et.advance(ntr_tot)
            pe_a.advance(TT)
            s_au.advance(TT)

            with nc.Block(f"gm1{li}") as blk:
                @blk.vector
                def _(v):
                    v.reduce_max(gmx[:, :], stmx[:, :], axis=AX.X)
            with nc.Block(f"gm2{li}") as blk:
                @blk.tensor
                def _(t):
                    t.transpose(pgm, gmx[:, :], ids_f[:, :])
            with nc.Block(f"gm3{li}") as blk:
                @blk.vector
                def _(v):
                    v.reduce_max(m11[:, :], pgm, axis=AX.X)
            with nc.Block(f"gm4{li}") as blk:
                @blk.tensor
                def _(t):
                    t.matmul(pbc1, ones_s[0:1, :], m11[0:1, :],
                             start=True, stop=True)
            with nc.Block(f"gm5{li}") as blk:
                @blk.vector
                def _(v):
                    v.tensor_scalar_mul(negM[:, :], pbc1, -1.0)
            with nc.Block(f"sm0{li}") as blk:
                @blk.scalar
                def _(s):
                    s.activation(fts[:, :], stmx[:, :], AF.Exp, bias=negM[:, 0:1])
            with nc.Block(f"sm1{li}") as blk:
                @blk.vector
                def _(v):
                    vch.cut(v.tensor_mul(gw8[:, :], fts[:, :], strs[:, :]))
                    vch.wait(v)
                    v.reduce_sum(gsm[:, :], gw8[:, :], axis=AX.X)
            with nc.Block(f"sm2{li}") as blk:
                @blk.tensor
                def _(t):
                    t.transpose(pgm, gsm[:, :], ids_f[:, :])
            with nc.Block(f"sm3{li}") as blk:
                @blk.vector
                def _(v):
                    v.reduce_sum(m11[:, :], pgm, axis=AX.X)
            with nc.Block(f"sm4{li}") as blk:
                @blk.tensor
                def _(t):
                    t.matmul(pbc1, ones_s[0:1, :], m11[0:1, :],
                             start=True, stop=True)
            with nc.Block(f"sm5{li}") as blk:
                @blk.vector
                def _(v):
                    vch.cut(v.reciprocal(sinv[:, :], pbc1))
                    vch.wait(v)
                    v.tensor_scalar(fts[:, :], fts[:, :], sinv[:, 0:1], None,
                                    ALU.mult)

            with nc.Block(f"bres{li}") as blk:
                @blk.sync
                def _(sy):
                    for i in range(TT + 2):
                        if i >= 2:
                            sy.wait_ge(v_res.h, v_res.v(i - 1))
                            b2 = (i - 2) % 2
                            sst[b2].inc(sy.dma_start(
                                xdst.ap()[(i - 2) * 128:(i - 1) * 128, :],
                                xnew[:, b2 * D:b2 * D + D]), 16)
                        if i < TT:
                            b = i % 2
                            sx2[b].inc(sy.dma_start(
                                xt2[:, b * D:b * D + D],
                                xsrc.ap()[i * 128:(i + 1) * 128, :]), 16)

                @blk.vector
                def _(v):
                    for i in range(TT):
                        b = i % 2
                        v.wait_ge(sx2[b].h, sx2[b].v(16 * (i // 2 + 1)))
                        if i >= 2:
                            v.wait_ge(sst[b].h, sst[b].v(16 * (i // 2)))
                        v_res.inc(v.scalar_tensor_tensor(
                            xnew[:, b * D:b * D + D], au[:, i * D:(i + 1) * D],
                            fts[:, i:i + 1], xt2[:, b * D:b * D + D],
                            ALU.mult, ALU.add))
            sx2[0].advance(16 * (TT // 2))
            sx2[1].advance(16 * (TT // 2))
            sst[0].advance(16 * (TT // 2))
            sst[1].advance(16 * (TT // 2))
            v_res.advance(TT)

        else:
            with nc.Block(f"acore{li}") as blk:
                def emit_att(t, i, tr0):
                    if i >= 1:
                        t.wait_ge(v_res.h, v_res.v(i))
                    for sc in range(TT):
                        t.wait_ge(s_et.h, s_et.v(tr0 + sc + 1))
                        for dc in range(2):
                            ins = t.matmul(att[:, dc * 512:(dc + 1) * 512],
                                           eT[:, sc * 128:(sc + 1) * 128],
                                           v_sb[:, sc * D + dc * 512:sc * D + (dc + 1) * 512],
                                           start=(sc == 0), stop=(sc == TT - 1))
                    pe_a.inc(ins)

                @blk.tensor
                def _(t):
                    trc = 0
                    for i in range(TT):
                        if i >= 1:
                            t.wait_ge(s_exp.h, s_exp.v(i))
                        pe_s.inc(emit_scores(t, i))
                        if i >= 1:
                            emit_att(t, i - 1, trc - TT)
                        t.wait_ge(s_exp.h, s_exp.v(i + 1))
                        for sc in range(TT):
                            if trc >= 2:
                                t.wait_ge(s_et.h, s_et.v(trc - 1))
                            pe_t.inc(t.transpose(psT2[trc % 2][:, :],
                                                 e_sb[:, sc * 128:(sc + 1) * 128],
                                                 ids_b[:, :]))
                            trc += 1
                    emit_att(t, TT - 1, trc - TT)

                def emit_stt(v, i):
                    b = i % 2
                    v.wait_ge(pe_a.h, pe_a.v(i + 1))
                    v.wait_ge(sx2[b].h, sx2[b].v(16 * (i // 2 + 1)))
                    if i >= 2:
                        v.wait_ge(sst[b].h, sst[b].v(16 * (i // 2)))
                    v.wait_ge(v_misc.h, v_misc.v(i + 1))
                    v_res.inc(v.scalar_tensor_tensor(
                        xnew[:, b * D:b * D + D], att[:, :],
                        winv[:, b:b + 1], xt2[:, b * D:b * D + D],
                        ALU.mult, ALU.add))

                @blk.vector
                def _(v):
                    for i in range(TT):
                        b = i % 2
                        v.wait_ge(pe_s.h, pe_s.v(i + 1))
                        v_max.inc(v.reduce_max(negm[:, b:b + 1], Sps[:, :],
                                               axis=AX.X, negate=True))
                        v.wait_ge(s_exp.h, s_exp.v(i + 1))
                        v_misc.inc(v.reciprocal(winv[:, b:b + 1],
                                                rsum[:, b:b + 1]))
                        if i >= 1:
                            emit_stt(v, i - 1)
                    emit_stt(v, TT - 1)

                @blk.scalar
                def _(s):
                    trc = 0
                    for i in range(TT):
                        b = i % 2
                        s.wait_ge(v_max.h, v_max.v(i + 1))
                        s_exp.inc(s.activation(e_sb[:, :], Sps[:, :], AF.Exp,
                                               bias=negm[:, b:b + 1],
                                               accum_out=rsum[:, b:b + 1]))
                        for sc in range(TT):
                            s.wait_ge(pe_t.h, pe_t.v(trc + 1))
                            s_et.inc(s.copy(eT[:, sc * 128:(sc + 1) * 128],
                                            psT2[trc % 2][:, :]))
                            trc += 1

                @blk.sync
                def _(sy):
                    for i in range(TT + 2):
                        if i >= 2:
                            sy.wait_ge(v_res.h, v_res.v(i - 1))
                            b2 = (i - 2) % 2
                            sst[b2].inc(sy.dma_start(
                                xdst.ap()[(i - 2) * 128:(i - 1) * 128, :],
                                xnew[:, b2 * D:b2 * D + D]), 16)
                        if i < TT:
                            b = i % 2
                            sx2[b].inc(sy.dma_start(
                                xt2[:, b * D:b * D + D],
                                xsrc.ap()[i * 128:(i + 1) * 128, :]), 16)

            pe_s.advance(TT)
            v_max.advance(TT)
            s_exp.advance(TT)
            pe_t.advance(TT * TT)
            s_et.advance(TT * TT)
            pe_a.advance(TT)
            v_misc.advance(TT)
            v_res.advance(TT)
            sx2[0].advance(16 * (TT // 2))
            sx2[1].advance(16 * (TT // 2))
            sst[0].advance(16 * (TT // 2))
            sst[1].advance(16 * (TT // 2))

        psc.close()
        sub.close()

    def emit_ffn(li, xsrc, xdst):
        sub = ExitStack()
        Wg_sb = [sub.enter_context(nc.sbuf_tensor(f"Wg{li}{i}", [128, HC * FCH], BF16))
                 for i in range(2)]
        Wu_sb = [sub.enter_context(nc.sbuf_tensor(f"Wu{li}{i}", [128, HC * FCH], BF16))
                 for i in range(2)]
        Wd_sb = [sub.enter_context(nc.sbuf_tensor(f"Wd{li}{i}", [128, HC * D], BF16))
                 for i in range(2)]
        xlnT = sub.enter_context(nc.sbuf_tensor(f"xlnTf{li}", [128, HC * TH], BF16))
        acc = sub.enter_context(nc.sbuf_tensor(f"acc{li}", [128, HC * TH], F32))
        hact = [sub.enter_context(nc.sbuf_tensor(f"hact{li}{i}", [128, HC * 512], BF16))
                for i in range(2)]
        cbB = [sub.enter_context(nc.sbuf_tensor(f"cbB{li}{i}", [128, TH], BF16))
               for i in range(2)]
        gg = sub.enter_context(nc.sbuf_tensor(f"gg{li}", [128, 2 * 512], BF16))

        swf = [S["swf0"], S["swf1"]]

        def issue_chunk(g, c, cb):
            e, fc = divmod(c, NFC)
            sw = swf[c % 2]
            sw.inc(g.dma_start(Wg_sb[cb][:, :], gch.ap()[e][fc]), 16)
            sw.inc(g.dma_start(Wu_sb[cb][:, :], uch.ap()[e][fc]), 16)
            sw.inc(g.dma_start(Wd_sb[cb][:, :], dch.ap()[e][fc]), 16)

        for half in range(2):
            goff = half * TH
            hl = f"{li}h{half}"

            with nc.Block(f"fw{hl}") as blk:
                @blk.gpsimd
                def _(g):
                    issue_chunk(g, 0, 0)
                    issue_chunk(g, 1, 1)

            psc = ExitStack()
            psT = [psc.enter_context(nc.psum_tensor(f"psTf{hl}{i}", [128, 128], BF16))
                   for i in range(2)]
            pr = psc.enter_context(nc.psum_tensor(f"pr{hl}", [128, 8], F32))
            pT8 = psc.enter_context(nc.psum_tensor(f"pT8{hl}", [8, 8 * 128], BF16))
            pe_r, v_rm, s_rexp, v_cmb, s_cb, pe_t8, v_t8 = (
                S["pe_r"], S["v_rm"], S["s_rexp"], S["v_cmb"], S["s_cb"],
                S["pe_t8"], S["v_t8"])

            def r_tensor(t, i):
                if i >= 1:
                    t.wait_ge(v_rm.h, v_rm.v(i))
                t.wait_ge(S["v_te"].h, S["v_te"].v((i + 1) * 4))
                t.wait_ge(S["s_te"].h, S["s_te"].v((i + 1) * 4))
                for hc in range(HC):
                    ins = t.matmul(pr[:, :],
                                   xlnT[:, hc * TH + i * 128:hc * TH + (i + 1) * 128],
                                   gw_s[:, hc * E:(hc + 1) * E],
                                   start=(hc == 0), stop=(hc == HC - 1))
                pe_r.inc(ins)
                if i >= 1:
                    t.wait_ge(s_cb.h, s_cb.v(i))
                    pe_t8.inc(t.transpose(pT8[0:8, (i - 1) * 128:i * 128],
                                          rc16[:, ((i - 1) % 2) * 8:((i - 1) % 2) * 8 + 8], ids_b[:, :]))

            def rt_tensor(t):
                t.wait_ge(s_cb.h, s_cb.v(THT))
                pe_t8.inc(t.transpose(pT8[0:8, (THT - 1) * 128:THT * 128],
                                      rc16[:, ((THT - 1) % 2) * 8:((THT - 1) % 2) * 8 + 8], ids_b[:, :]))

            def r_partA(v, i):
                b = i % 2
                q1 = rq1[:, b * 8:(b + 1) * 8]
                q2 = rq2[:, b * 8:(b + 1) * 8]
                v.wait_ge(pe_r.h, pe_r.v(i + 1))
                if i >= 2:
                    v.wait_ge(s_rexp.h, s_rexp.v(i - 1))
                vch.cut(v.reduce_max(rm1[:, b:b + 1], pr[:, :], axis=AX.X))
                vch.wait(v)
                vch.cut(v.tensor_scalar(q1, pr[:, :], rm1[:, b:b + 1], None,
                                        ALU.is_equal))
                vch.wait(v)
                vch.cut(v.tensor_scalar_mul(q2, q1, 1.0e30))
                vch.wait(v)
                vch.cut(v.tensor_sub(q2, pr[:, :], q2))
                vch.wait(v)
                vch.cut(v.reduce_max(rm2[:, b:b + 1], q2, axis=AX.X))
                vch.wait(v)
                vch.cut(v.tensor_scalar(q2, q2, rm2[:, b:b + 1], None,
                                        ALU.is_equal))
                v_rm.inc(v.tensor_sub(rden[:, b:b + 1], rm1[:, b:b + 1],
                                      rm2[:, b:b + 1]))

            def r_partB(v, i):
                b = i % 2
                v.wait_ge(s_rexp.h, s_rexp.v(i + 1))
                if i >= 1:
                    v.wait_ge(s_cb.h, s_cb.v(i))
                v.tensor_scalar(rw2[:, b:b + 1], rw1[:, b:b + 1], -1.0, 1.0,
                                ALU.mult, ALU.add)
                vch.cut(v.tensor_scalar(rcmb[:, :], rq1[:, b * 8:(b + 1) * 8],
                                        rw1[:, b:b + 1], None, ALU.mult))
                vch.wait(v)
                v_cmb.inc(v.scalar_tensor_tensor(
                    rcmb[:, :], rq2[:, b * 8:(b + 1) * 8], rw2[:, b:b + 1],
                    rcmb[:, :], ALU.mult, ALU.add))

            def r_vector(v, i):
                if i >= 1:
                    r_partB(v, i - 1)
                r_partA(v, i)

            def rt_vector(v):
                r_partB(v, THT - 1)
                v.wait_ge(pe_t8.h, pe_t8.v(THT))
                v_t8.inc(v.tensor_copy(combT8[0:8, :], pT8[0:8, :]))

            def r_scalar(s, i):
                if i >= 1:
                    s.wait_ge(v_cmb.h, v_cmb.v(i))
                    s_cb.inc(s.copy(rc16[:, ((i - 1) % 2) * 8:((i - 1) % 2) * 8 + 8], rcmb[:, :]))
                s.wait_ge(v_rm.h, v_rm.v(i + 1))
                s_rexp.inc(s.activation(rw1[:, i % 2:i % 2 + 1],
                                        rden[:, i % 2:i % 2 + 1], AF.Sigmoid))

            def rt_scalar(s):
                s.wait_ge(v_cmb.h, v_cmb.v(THT))
                s_cb.inc(s.copy(rc16[:, ((THT - 1) % 2) * 8:((THT - 1) % 2) * 8 + 8], rcmb[:, :]))

            emit_ln(f"f{hl}", xsrc, THT, goff, xlnT, TH, psT,
                    ext={"tile_tensor": r_tensor, "tail_tensor": rt_tensor,
                         "tile_vector": r_vector, "tail_vector": rt_vector,
                         "tile_scalar": r_scalar, "tail_scalar": rt_scalar})

            pe_r.advance(THT)
            v_rm.advance(THT)
            s_rexp.advance(THT)
            v_cmb.advance(THT)
            s_cb.advance(THT)
            pe_t8.advance(THT)
            v_t8.advance(1)
            psc.close()

            psc = ExitStack()
            pg = [psc.enter_context(nc.psum_tensor(f"pg{hl}{i}", [128, 512], F32))
                  for i in range(2)]
            pu = [psc.enter_context(nc.psum_tensor(f"pu{hl}{i}", [128, 512], F32))
                  for i in range(2)]
            pd = [psc.enter_context(nc.psum_tensor(f"pd{hl}{i}", [128, 512], F32))
                  for i in range(2)]
            pcb = psc.enter_context(nc.psum_tensor(f"pcb{hl}", [128, TH], F32))

            pe_cb, v_cb, pe_g, pe_u, s_act, v_hact, pe_d, v_ev = (
                S["pe_cb"], S["v_cb"], S["pe_g"], S["pe_u"], S["s_act"],
                S["v_hact"], S["pe_d"], S["v_ev"])

            with nc.Block(f"f2{hl}") as blk:
                @blk.tensor
                def _(t):
                    for c in range(NCH):
                        e, fc = divmod(c, NFC)
                        if fc == 0:
                            if e >= 1:
                                t.wait_ge(v_cb.h, v_cb.v(e))
                            for tc in range(2):
                                ins = t.matmul(pcb[:, tc * 512:(tc + 1) * 512],
                                               sel_s[0:8, e * 128:(e + 1) * 128],
                                               combT8[0:8, tc * 512:(tc + 1) * 512],
                                               start=True, stop=True)
                            pe_cb.inc(ins)
                        t.wait_ge(swf[c % 2].h, swf[c % 2].v(48 * (c // 2 + 1)))
                        for tb in range(2):
                            for j in range(HC):
                                gj = (c * 2 + tb) * HC + j
                                if gj >= 2:
                                    t.wait_ge(s_act.h, s_act.v(gj - 1))
                                for hc in range(HC):
                                    ins = t.matmul(pg[j % 2][:, :],
                                                   Wg_sb[c % 2][:, hc * FCH + j * 128:hc * FCH + (j + 1) * 128],
                                                   xlnT[:, hc * TH + tb * 512:hc * TH + (tb + 1) * 512],
                                                   start=(hc == 0), stop=(hc == HC - 1))
                                pe_g.inc(ins)
                                if gj >= 2:
                                    t.wait_ge(S["v_h1"].h, S["v_h1"].v(gj - 1))
                                for hc in range(HC):
                                    ins = t.matmul(pu[j % 2][:, :],
                                                   Wu_sb[c % 2][:, hc * FCH + j * 128:hc * FCH + (j + 1) * 128],
                                                   xlnT[:, hc * TH + tb * 512:hc * TH + (tb + 1) * 512],
                                                   start=(hc == 0), stop=(hc == HC - 1))
                                pe_u.inc(ins)
                            for h2 in range(HC):
                                gdh = (c * 2 + tb) * HC + h2
                                if gdh >= 2:
                                    t.wait_ge(v_ev.h, v_ev.v(gdh - 1))
                                for j in range(HC):
                                    if h2 == 0:
                                        t.wait_ge(v_hact.h,
                                                  v_hact.v((c * 2 + tb) * HC + j + 1))
                                    ins = t.matmul(pd[h2 % 2][:, :],
                                                   Wd_sb[c % 2][:, j * D + h2 * 128:j * D + (h2 + 1) * 128],
                                                   hact[tb][:, j * 512:(j + 1) * 512],
                                                   start=(j == 0), stop=(j == HC - 1))
                                pe_d.inc(ins)

                @blk.scalar
                def _(s):
                    for c in range(NCH):
                        for tb in range(2):
                            for j in range(HC):
                                gj = (c * 2 + tb) * HC + j
                                s.wait_ge(pe_g.h, pe_g.v(gj + 1))
                                if gj >= 2:
                                    s.wait_ge(S["v_h1"].h, S["v_h1"].v(gj - 1))
                                s_act.inc(s.activation(
                                    gg[:, (j % 2) * 512:(j % 2) * 512 + 512],
                                    pg[j % 2][:, :], AF.Gelu))

                @blk.vector
                def _(v):
                    for c in range(NCH):
                        e, fc = divmod(c, NFC)
                        if fc == 0:
                            v.wait_ge(pe_cb.h, pe_cb.v(e + 1))
                            v_cb.inc(v.tensor_copy(cbB[e % 2][:, :], pcb[:, :]))
                        for tb in range(2):
                            for j in range(HC):
                                gj = (c * 2 + tb) * HC + j
                                v.wait_ge(s_act.h, s_act.v(gj + 1))
                                v.wait_ge(pe_u.h, pe_u.v(gj + 1))
                                if c >= 1:
                                    v.wait_ge(pe_d.h,
                                              pe_d.v(((c - 1) * 2 + tb + 1) * HC))
                                S["v_h1"].inc(v.tensor_mul(
                                    hact[tb][:, j * 512:(j + 1) * 512],
                                    pu[j % 2][:, :],
                                    gg[:, (j % 2) * 512:(j % 2) * 512 + 512]))
                                v.wait_ge(S["v_h1"].h, S["v_h1"].v(gj + 1))
                                v_hact.inc(v.tensor_mul(
                                    hact[tb][:, j * 512:(j + 1) * 512],
                                    hact[tb][:, j * 512:(j + 1) * 512],
                                    cbB[e % 2][:, tb * 512:(tb + 1) * 512]))
                            for h2 in range(HC):
                                gdh = (c * 2 + tb) * HC + h2
                                v.wait_ge(pe_d.h, pe_d.v(gdh + 1))
                                dst = acc[:, h2 * TH + tb * 512:h2 * TH + (tb + 1) * 512]
                                if c == 0:
                                    v_ev.inc(v.tensor_copy(dst, pd[h2 % 2][:, :]))
                                else:
                                    v_ev.inc(v.tensor_tensor(dst, pd[h2 % 2][:, :],
                                                             dst, ALU.add))

                @blk.gpsimd
                def _(g):
                    for c in range(2, NCH):
                        g.wait_ge(pe_d.h, pe_d.v((c - 1) * 2 * HC))
                        issue_chunk(g, c, c % 2)

            pe_cb.advance(E)
            v_cb.advance(E)
            pe_g.advance(NCH * 2 * HC)
            pe_u.advance(NCH * 2 * HC)
            s_act.advance(NCH * 2 * HC)
            S["v_h1"].advance(NCH * 2 * HC)
            v_hact.advance(NCH * 2 * HC)
            pe_d.advance(NCH * 2 * HC)
            v_ev.advance(NCH * 2 * HC)
            swf[0].advance(48 * (NCH // 2))
            swf[1].advance(48 * (NCH // 2))
            psc.close()

            psc = ExitStack()
            pF3 = [psc.enter_context(nc.psum_tensor(f"pF3{hl}{i}", [128, 128], F32))
                   for i in range(2)]
            pe_t, v_f3 = S["pe_t"], S["v_f3"]
            sx2 = [S["sx2_0"], S["sx2_1"]]
            sst = [S["st0"], S["st1"]]
            with nc.Block(f"f3{hl}") as blk:
                @blk.tensor
                def _(t):
                    trc = 0
                    for i in range(THT):
                        for j in range(HC):
                            if trc >= 2:
                                t.wait_ge(v_f3.h, v_f3.v(trc - 1))
                            pe_t.inc(t.transpose(
                                pF3[trc % 2][:, :],
                                acc[:, j * TH + i * 128:j * TH + (i + 1) * 128],
                                ids_f[:, :]))
                            trc += 1

                @blk.vector
                def _(v):
                    trc = 0
                    for i in range(THT):
                        b = i % 2
                        v.wait_ge(sx2[b].h, sx2[b].v(16 * (i // 2 + 1)))
                        if i >= 2:
                            v.wait_ge(sst[b].h, sst[b].v(16 * (i // 2)))
                        for j in range(HC):
                            v.wait_ge(pe_t.h, pe_t.v(trc + 1))
                            v_f3.inc(v.tensor_tensor(
                                xnew[:, b * D + j * 128:b * D + (j + 1) * 128],
                                pF3[trc % 2][:, :],
                                xt2[:, b * D + j * 128:b * D + (j + 1) * 128],
                                ALU.add))
                            trc += 1

                @blk.sync
                def _(sy):
                    for i in range(THT + 2):
                        if i >= 2:
                            sy.wait_ge(v_f3.h, v_f3.v(HC * (i - 1)))
                            b2 = (i - 2) % 2
                            sst[b2].inc(sy.dma_start(
                                xdst.ap()[goff + (i - 2) * 128:goff + (i - 1) * 128, :],
                                xnew[:, b2 * D:b2 * D + D]), 16)
                        if i < THT:
                            b = i % 2
                            sx2[b].inc(sy.dma_start(
                                xt2[:, b * D:b * D + D],
                                xsrc.ap()[goff + i * 128:goff + (i + 1) * 128, :]), 16)

            pe_t.advance(THT * HC)
            v_f3.advance(THT * HC)
            sx2[0].advance(16 * ((THT + 1) // 2))
            sx2[1].advance(16 * (THT // 2))
            sst[0].advance(16 * ((THT + 1) // 2))
            sst[1].advance(16 * (THT // 2))
            psc.close()

        sub.close()

    chain = [
        ("a", xin, out if n_sub == 1 else sp0),
        ("f", sp0, out if n_sub == 2 else sp1),
        ("b", sp1, out if n_sub == 3 else sp0),
        ("f", sp0, out),
    ][:n_sub]

    for li, (kind, src, dst) in enumerate(chain):
        if kind == "f":
            emit_ffn(li, src, dst)
        else:
            emit_attn(kind, li, src, dst)

    with nc.Block("fin") as blk:
        @blk.sync
        def _(sy):
            sy.wait_ge(S["st0"].h, S["st0"].v(0))
            sy.wait_ge(S["st1"].h, S["st1"].v(0))

    st.close()
    return nc


def _prep_inputs(kw):
    f32 = np.float32
    x = np.asarray(kw["L0_attn_a_x"], dtype=f32)
    c = np.ascontiguousarray
    def attn_w(w_hd):
        return c(w_hd.reshape(HC, 128, D).transpose(1, 0, 2).reshape(128, HC * D))

    gup = np.asarray(kw["L0_ffn_gate_up_w"], f32)
    dw = np.asarray(kw["L0_ffn_down_w"], f32)
    def chunked_gu(part):
        t = part.reshape(E, NFC, FCH, HC, 128)
        return c(t.transpose(0, 1, 4, 3, 2).reshape(E, NFC, 128, HC * FCH))
    dt_ = dw.reshape(E, D, NFC, HC, 128)
    dch_np = c(dt_.transpose(0, 2, 4, 3, 1).reshape(E, NFC, 128, HC * D))
    base = {
        "aWqT": attn_w(np.asarray(kw["L0_attn_a_Wq"], f32).T),
        "aWkT": attn_w(np.asarray(kw["L0_attn_a_Wk"], f32).T),
        "aWvT": attn_w(np.asarray(kw["L0_attn_a_Wv"], f32).T),
        "bWq": attn_w(np.asarray(kw["L0_attn_b_Wq"], f32)),
        "bWk": attn_w(np.asarray(kw["L0_attn_b_Wk"], f32)),
        "bWv": attn_w(np.asarray(kw["L0_attn_b_Wv"], f32)),
        "bqt": c(np.asarray(kw["L0_attn_b_bq"], f32).reshape(HC, 128).T),
        "bkt": c(np.asarray(kw["L0_attn_b_bk"], f32).reshape(HC, 128).T),
        "bvr": c(np.asarray(kw["L0_attn_b_bv"], f32).reshape(1, D)),
        "gwT": c(np.asarray(kw["L0_ffn_gate_w"], f32).T),
        "gch": chunked_gu(gup[:, :FF, :]),
        "uch": chunked_gu(gup[:, FF:, :]),
        "dch": dch_np,
    }
    in_maps = []
    for core in range(8):
        m = dict(base)
        m["xin"] = c(x[core])
        in_maps.append(m)
    return in_maps


def _run(kw, n_sub=4):
    in_maps = _prep_inputs(kw)
    nc = build_program(n_sub)
    res = run_bass_kernel_spmd(nc, in_maps, core_ids=list(range(8)))
    return np.stack([r["out"] for r in res.results], axis=0)


def kernel(**kw):
    return _run(kw, 4)


# version 08 (final)
# speedup vs baseline: 1.1031x; 1.0325x over previous
import numpy as np
import ml_dtypes
from contextlib import ExitStack

import concourse.bass as bass
import concourse.mybir as mybir
from concourse.bass_utils import run_bass_kernel_spmd

F32 = mybir.dt.float32
BF16 = mybir.dt.bfloat16
AF = mybir.ActivationFunctionType
ALU = mybir.AluOpType
AX = mybir.AxisListType

D = 1024
T = 2048
TT = T // 128
HC = D // 128
E = 8
FF = 4096
TH = T // 2
THT = TH // 128
NF_A = 1.0 / 32.0
SC_B = 1.0 / 32.0
LN_EPS = 1e-5
NEG = -1.0e30

FCH = 1024
NFC = FF // FCH
NCH = E * NFC


class SemC:

    def __init__(self, stack, nc, name):
        self.h = stack.enter_context(nc.semaphore(name))
        self.base = 0
        self.emitted = 0

    def inc(self, ins, n=1):
        self.emitted += n
        return ins.then_inc(self.h, n)

    def advance(self, total):
        assert self.emitted == total, (self.h, self.emitted, total)
        self.base += total
        self.emitted = 0

    def v(self, k):
        return self.base + k


class ChainEdge:

    def __init__(self, stack, nc, name):
        self.h = stack.enter_context(nc.semaphore(name))
        self.n = 0

    def cut(self, ins):
        self.n += 1
        ins.then_inc(self.h, 1)
        return ins

    def wait(self, eng):
        eng.wait_ge(self.h, self.n)


def build_program(n_sub=4):
    nc = bass.Bass("TRN2", target_bir_lowering=False)
    st = ExitStack()

    xin = nc.dram_tensor("xin", [T, D], F32, kind="ExternalInput")
    out = nc.dram_tensor("out", [T, D], F32, kind="ExternalOutput")
    sp0 = nc.dram_tensor("sp0", [T, D], F32, kind="Internal")
    sp1 = nc.dram_tensor("sp1", [T, D], F32, kind="Internal")
    aWqT = nc.dram_tensor("aWqT", [128, HC * D], F32, kind="ExternalInput")
    aWkT = nc.dram_tensor("aWkT", [128, HC * D], F32, kind="ExternalInput")
    aWvT = nc.dram_tensor("aWvT", [128, HC * D], F32, kind="ExternalInput")
    bWq = nc.dram_tensor("bWq", [128, HC * D], F32, kind="ExternalInput")
    bWk = nc.dram_tensor("bWk", [128, HC * D], F32, kind="ExternalInput")
    bWv = nc.dram_tensor("bWv", [128, HC * D], F32, kind="ExternalInput")
    bqt = nc.dram_tensor("bqt", [128, HC], F32, kind="ExternalInput")
    bkt = nc.dram_tensor("bkt", [128, HC], F32, kind="ExternalInput")
    bvr = nc.dram_tensor("bvr", [1, D], F32, kind="ExternalInput")
    gwT = nc.dram_tensor("gwT", [D, E], F32, kind="ExternalInput")
    gch = nc.dram_tensor("gch", [E, NFC, 128, HC * FCH], F32, kind="ExternalInput")
    uch = nc.dram_tensor("uch", [E, NFC, 128, HC * FCH], F32, kind="ExternalInput")
    dch = nc.dram_tensor("dch", [E, NFC, 128, HC * D], F32, kind="ExternalInput")

    ident_b = nc.inline_tensor(np.eye(128, dtype=ml_dtypes.bfloat16), name="identb")
    ident_f = nc.inline_tensor(np.eye(128, dtype=np.float32), name="identf")
    cam_np = np.where(np.arange(128)[:, None] >= np.arange(128)[None, :],
                      0.0, NEG).astype(np.float32)
    cam_c = nc.inline_tensor(cam_np, name="camc")
    sel_np = np.zeros((8, 8 * 128), dtype=ml_dtypes.bfloat16)
    for e in range(8):
        sel_np[e, e * 128:(e + 1) * 128] = 1.0
    sel_c = nc.inline_tensor(sel_np, name="selc")
    ones_c = nc.inline_tensor(np.ones((1, 128), dtype=np.float32), name="onesc")

    A = nc.alloc_sbuf_tensor
    ids_b = A("ids_b", [128, 128], BF16)
    ids_f = A("ids_f", [128, 128], F32)
    cam_s = A("cam_s", [128, 128], F32)
    sel_s = A("sel_s", [8, 8 * 128], BF16)
    ones_s = A("ones_s", [1, 128], F32)
    bq_s = A("bq_s", [128, HC], F32)
    bk32 = A("bk32", [128, HC], F32)
    bvB = A("bvB", [128, D], F32)
    gw_s = A("gw_s", [128, HC * E], BF16)
    xt = A("xt", [128, 2 * D], F32)
    xt2 = A("xt2", [128, 2 * D], F32)
    xnew = A("xnew", [128, 2 * D], F32)
    xlt = A("xlt", [128, 2 * D], BF16)
    lsum = A("lsum", [128, 2], F32)
    lsq = A("lsq", [128, 2], F32)
    lmean = A("lmean", [128, 2], F32)
    lmsq = A("lmsq", [128, 2], F32)
    lm2 = A("lm2", [128, 2], F32)
    lvar = A("lvar", [128, 2], F32)
    llog = A("llog", [128, 2], F32)
    lrstd = A("lrstd", [128, 2], F32)
    lnegmb = A("lnegmb", [128, 2], F32)
    negm = A("negm", [128, 2], F32)
    rsum = A("rsum", [128, 2], F32)
    winv = A("winv", [128, 2], F32)
    stmx = A("stmx", [128, TT], F32)
    strs = A("strs", [128, TT], F32)
    gmx = A("gmx", [128, 1], F32)
    fts = A("fts", [128, TT], F32)
    gw8 = A("gw8", [128, TT], F32)
    gsm = A("gsm", [128, 1], F32)
    m11 = A("m11", [1, 1], F32)
    negM = A("negM", [128, 1], F32)
    sinv = A("sinv", [128, 1], F32)
    rmx = A("rmx", [128, 2], F32)
    re8 = A("re8", [128, 8], F32)
    rq1 = A("rq1", [128, 2 * 8], F32)
    rq2 = A("rq2", [128, 2 * 8], F32)
    rm1 = A("rm1", [128, 2], F32)
    rm2 = A("rm2", [128, 2], F32)
    rden = A("rden", [128, 2], F32)
    rw1 = A("rw1", [128, 2], F32)
    rw2 = A("rw2", [128, 2], F32)
    rcmb = A("rcmb", [128, 8], F32)
    rc16 = A("rc16", [128, 2 * 8], BF16)
    combT8 = A("combT8", [8, TH], BF16)

    sem_names = [
        "sx0", "sx1", "sx2_0", "sx2_1", "st0", "st1", "swa", "swf0", "swf1",
        "s_sq", "v_st", "s_rs", "v_nm", "s_norm", "pe_t", "v_te", "s_te",
        "pe_r", "v_rm", "s_rexp", "v_se", "v_cmb", "s_cb", "pe_t8", "v_t8",
        "pe_q", "s_qe", "v_qe", "pe_bc", "v_bc",
        "pe_s", "v_max", "s_exp", "v_misc", "s_et", "pe_a", "v_res", "s_au",
        "pe_cb", "v_cb", "pe_g", "pe_u", "s_act", "v_hact", "pe_d", "v_ev",
        "v_f3", "v_h1",
    ]
    S = {n: SemC(st, nc, n) for n in sem_names}
    vch = ChainEdge(st, nc, "vch")
    sch = ChainEdge(st, nc, "sch")
    wcnt = {"n": 0}

    with nc.Block("init") as blk:
        @blk.sync
        def _(sy):
            for dst, src in ((ids_b, ident_b), (ids_f, ident_f), (cam_s, cam_c),
                             (sel_s, sel_c), (ones_s, ones_c), (bq_s, bqt),
                             (bk32, bkt)):
                S["sx0"].inc(sy.dma_start(dst[:, :], src.ap()), 16)
            S["sx0"].inc(sy.dma_start(bvB[0:1, :], bvr.ap()), 16)
            sy.wait_ge(S["sx0"].h, S["sx0"].v(16 * 8))

        @blk.gpsimd
        def _(g):
            src = gwT.ap().rearrange("(hc p) e -> p hc e", p=128)
            dst = gw_s[:, :].rearrange("p (hc e) -> p hc e", hc=HC)
            S["swa"].inc(g.dma_start(dst, src), 16)
            g.wait_ge(S["swa"].h, S["swa"].v(16))
    S["sx0"].advance(8 * 16)
    S["swa"].advance(16)

    with nc.Block("bk32b") as blk:
        @blk.vector
        def _(v):
            v.tensor_scalar_mul(bk32[:, :], bk32[:, :], SC_B)

    def emit_ln(tag, xsrc, ntiles, goff, xlnT, Tlen, psT, ext=None):
        s_sq, v_st, s_rs, v_nm, s_norm, pe_t, v_te, s_te = (
            S["s_sq"], S["v_st"], S["s_rs"], S["v_nm"], S["s_norm"],
            S["pe_t"], S["v_te"], S["s_te"])
        sx = [S["sx0"], S["sx1"]]
        ext = ext or {}

        def hook(eng, name, i=None):
            fn = ext.get(name)
            if fn is not None:
                fn(eng) if i is None else fn(eng, i)

        with nc.Block(f"ln{tag}") as blk:
            @blk.sync
            def _(sy):
                sy.wait_ge(S["st0"].h, S["st0"].v(0))
                sy.wait_ge(S["st1"].h, S["st1"].v(0))
                for i in range(ntiles):
                    if i >= 2:
                        sy.wait_ge(s_norm.h, s_norm.v(i - 1))
                    sx[i % 2].inc(sy.dma_start(
                        xt[:, (i % 2) * D:(i % 2) * D + D],
                        xsrc.ap()[goff + i * 128:goff + (i + 1) * 128, :]), 16)
                hook(sy, "tail_sync")

            if "tail_gpsimd" in ext:
                @blk.gpsimd
                def _(g):
                    hook(g, "tail_gpsimd")

            @blk.vector
            def _(v):
                for i in range(ntiles):
                    b = i % 2
                    v.wait_ge(sx[b].h, sx[b].v(16 * (i // 2 + 1)))
                    vch.cut(v.reduce_sum(lsum[:, b:b + 1], xt[:, b * D:b * D + D],
                                         axis=AX.X))
                    v.wait_ge(s_sq.h, s_sq.v(i + 1))
                    vch.wait(v)
                    v.tensor_scalar_mul(lmean[:, b:b + 1], lsum[:, b:b + 1], 1.0 / D)
                    vch.cut(v.tensor_scalar_mul(lmsq[:, b:b + 1], lsq[:, b:b + 1],
                                                1.0 / D))
                    vch.wait(v)
                    vch.cut(v.tensor_mul(lm2[:, b:b + 1], lmean[:, b:b + 1],
                                         lmean[:, b:b + 1]))
                    vch.wait(v)
                    vch.cut(v.tensor_sub(lvar[:, b:b + 1], lmsq[:, b:b + 1],
                                         lm2[:, b:b + 1]))
                    vch.wait(v)
                    v_st.inc(v.tensor_scalar_add(lvar[:, b:b + 1],
                                                 lvar[:, b:b + 1], LN_EPS))
                    v.wait_ge(s_rs.h, s_rs.v(i + 1))
                    v_nm.inc(v.scalar_tensor_tensor(
                        lnegmb[:, b:b + 1], lmean[:, b:b + 1], -1.0,
                        lrstd[:, b:b + 1], ALU.mult, ALU.mult))
                    for hc in range(0, HC, 2):
                        k = i * HC + hc
                        v.wait_ge(pe_t.h, pe_t.v(k + 1))
                        v_te.inc(v.tensor_copy(
                            xlnT[:, hc * Tlen + i * 128:hc * Tlen + (i + 1) * 128],
                            psT[hc % 2][:, :]))
                    hook(v, "tile_vector", i)
                hook(v, "tail_vector")

            @blk.scalar
            def _(s):
                for i in range(ntiles):
                    b = i % 2
                    s.wait_ge(sx[b].h, sx[b].v(16 * (i // 2 + 1)))
                    s_sq.inc(s.activation(xnew[:, 0:D], xt[:, b * D:b * D + D],
                                          AF.Square, accum_out=lsq[:, b:b + 1]))
                    s.wait_ge(v_st.h, v_st.v(i + 1))
                    sch.cut(s.activation(llog[:, b:b + 1], lvar[:, b:b + 1], AF.Ln))
                    sch.wait(s)
                    s_rs.inc(s.activation(lrstd[:, b:b + 1], llog[:, b:b + 1],
                                          AF.Exp, scale=-0.5))
                    s.wait_ge(s_rs.h, s_rs.v(i + 1))
                    s.wait_ge(v_nm.h, v_nm.v(i + 1))
                    s_norm.inc(s.activation(xlt[:, b * D:b * D + D],
                                            xt[:, b * D:b * D + D], AF.Identity,
                                            scale=lrstd[:, b:b + 1],
                                            bias=lnegmb[:, b:b + 1]))
                    for hc in range(1, HC, 2):
                        k = i * HC + hc
                        s.wait_ge(pe_t.h, pe_t.v(k + 1))
                        s_te.inc(s.copy(
                            xlnT[:, hc * Tlen + i * 128:hc * Tlen + (i + 1) * 128],
                            psT[hc % 2][:, :]))
                    hook(s, "tile_scalar", i)
                hook(s, "tail_scalar")

            @blk.tensor
            def _(t):
                for i in range(ntiles):
                    b = i % 2
                    t.wait_ge(s_norm.h, s_norm.v(i + 1))
                    for hc in range(HC):
                        k = i * HC + hc
                        if k >= 2:
                            pk = k - 2
                            if pk % 2 == 0:
                                t.wait_ge(v_te.h, v_te.v(pk // 2 + 1))
                            else:
                                t.wait_ge(s_te.h, s_te.v(pk // 2 + 1))
                        pe_t.inc(t.transpose(
                            psT[hc % 2][:, :],
                            xlt[:, b * D + hc * 128:b * D + (hc + 1) * 128],
                            ids_b[:, :]))
                    hook(t, "tile_tensor", i)
                hook(t, "tail_tensor")

        sx[0].advance(16 * ((ntiles + 1) // 2))
        sx[1].advance(16 * (ntiles // 2))
        s_sq.advance(ntiles)
        v_st.advance(ntiles)
        s_rs.advance(ntiles)
        v_nm.advance(ntiles)
        s_norm.advance(ntiles)
        pe_t.advance(ntiles * HC)
        v_te.advance(ntiles * HC // 2)
        s_te.advance(ntiles * HC // 2)

    def emit_attn(kind, li, xsrc, xdst):
        isb = kind == "b"
        sub = ExitStack()
        subR = ExitStack()
        qT = sub.enter_context(nc.sbuf_tensor(f"qT{li}", [128, HC * T], BF16))
        kT = sub.enter_context(nc.sbuf_tensor(f"kT{li}", [128, HC * T], BF16))
        v_sb = sub.enter_context(nc.sbuf_tensor(f"v{li}", [128, TT * D], BF16))
        xlnT = subR.enter_context(
            nc.sbuf_tensor(f"xlnTa{li}", [128, HC * T], BF16, side="right"))
        W_ab = [subR.enter_context(
            nc.sbuf_tensor(f"Wab{li}_{i}", [128, HC * D], BF16, side="right"))
            for i in range(2)]

        wq, wk, wv = (aWqT, aWkT, aWvT) if kind == "a" else (bWq, bWk, bWv)
        sw, pe_q, s_qe, v_qe, pe_bc, v_bc = (S["swa"], S["pe_q"], S["s_qe"],
                                             S["v_qe"], S["pe_bc"], S["v_bc"])

        def wdma(g, wt, dst):
            sw.inc(g.dma_start(dst[:, :], wt.ap()), 16)

        with nc.Block(f"aw{li}") as blk:
            @blk.gpsimd
            def _(g):
                wdma(g, wq, W_ab[0])
                wdma(g, wk, W_ab[1])

        psc = ExitStack()
        psT = [psc.enter_context(nc.psum_tensor(f"psTa{li}{i}", [128, 128], BF16))
               for i in range(2)]
        pqk = [psc.enter_context(nc.psum_tensor(f"pqk{li}{i}", [128, 512], F32))
               for i in range(2)]
        pbc = psc.enter_context(nc.psum_tensor(f"pbc{li}", [128, D], F32))

        jobs = ([("q", dt, tc) for tc in range(4) for dt in range(HC)]
                + [("k", dt, tc) for tc in range(4) for dt in range(HC)]
                + [("v", s_, dc) for s_ in range(TT) for dc in range(2)])
        NQ = HC * 4

        def emit_job(t, jn):
            kj, a, b = jobs[jn]
            pb = jn % 2
            if jn == 0:
                t.wait_ge(sw.h, sw.v(32))
            if jn == 2 * NQ:
                t.wait_ge(sw.h, sw.v(48))
            if jn >= 2:
                pj = jn - 2
                if jobs[pj][0] == "v":
                    t.wait_ge(v_qe.h, v_qe.v(pj - 2 * NQ + 1))
                else:
                    t.wait_ge(s_qe.h, s_qe.v(pj + 1))
            need = (4 * b + 4 if kj in ("q", "k") else a + 1) * 4
            t.wait_ge(S["v_te"].h, S["v_te"].v(need))
            t.wait_ge(S["s_te"].h, S["s_te"].v(need))
            for hc in range(HC):
                if kj == "q":
                    ins = t.matmul(pqk[pb][:, :],
                                   W_ab[0][:, hc * D + a * 128:hc * D + (a + 1) * 128],
                                   xlnT[:, hc * T + b * 512:hc * T + (b + 1) * 512],
                                   start=(hc == 0), stop=(hc == HC - 1))
                elif kj == "k":
                    ins = t.matmul(pqk[pb][:, :],
                                   W_ab[1][:, hc * D + a * 128:hc * D + (a + 1) * 128],
                                   xlnT[:, hc * T + b * 512:hc * T + (b + 1) * 512],
                                   start=(hc == 0), stop=(hc == HC - 1))
                else:
                    ins = t.matmul(pqk[pb][:, :],
                                   xlnT[:, hc * T + a * 128:hc * T + (a + 1) * 128],
                                   W_ab[0][:, hc * D + b * 512:hc * D + (b + 1) * 512],
                                   start=(hc == 0), stop=(hc == HC - 1))
            pe_q.inc(ins)

        def emit_evac(eng, jn):
            kj, a, b = jobs[jn]
            pb = jn % 2
            eng.wait_ge(pe_q.h, pe_q.v(jn + 1))
            if kj == "v":
                col = a * D + b * 512
                if isb:
                    v_qe.inc(eng.tensor_add(v_sb[:, col:col + 512],
                                            pqk[pb][:, :],
                                            bvB[:, b * 512:b * 512 + 512]))
                else:
                    v_qe.inc(eng.tensor_scalar_mul(v_sb[:, col:col + 512],
                                                   pqk[pb][:, :], NF_A))
                return
            dst = qT if kj == "q" else kT
            col = a * T + b * 512
            if isb:
                bias = (bq_s if kj == "q" else bk32)[:, a:a + 1]
                scale = 1.0 if kj == "q" else SC_B
                s_qe.inc(eng.activation(dst[:, col:col + 512], pqk[pb][:, :],
                                        AF.Identity, bias=bias, scale=scale))
            else:
                s_qe.inc(eng.copy(dst[:, col:col + 512], pqk[pb][:, :]))

        st_pe = {"n": 0}
        st_sc = {"n": 0}

        def tile_tensor(t, i):
            if isb and i == 0:
                for dc in range(2):
                    pe_bc.inc(t.matmul(pbc[:, dc * 512:dc * 512 + 512],
                                       ones_s[0:1, :],
                                       bvB[0:1, dc * 512:dc * 512 + 512],
                                       start=True, stop=True))
            if i >= 3:
                hi = min(2 * (i - 2), NQ)
                while st_pe["n"] < hi:
                    emit_job(t, st_pe["n"])
                    st_pe["n"] += 1

        def tail_tensor(t):
            for jn in range(st_pe["n"], len(jobs)):
                emit_job(t, jn)

        def tile_scalar(s, i):
            if i >= 4:
                hi = min(2 * (i - 3), NQ)
                while st_sc["n"] < hi:
                    emit_evac(s, st_sc["n"])
                    st_sc["n"] += 1

        def tail_scalar(s):
            for jn in range(st_sc["n"], 2 * NQ):
                emit_evac(s, jn)

        def tail_vector(v):
            if isb:
                for dc in range(2):
                    v.wait_ge(pe_bc.h, pe_bc.v(dc + 1))
                    vch.cut(v.tensor_copy(
                        bvB[:, dc * 512:dc * 512 + 512],
                        pbc[:, dc * 512:dc * 512 + 512]))
                vch.wait(v)
            for jn in range(2 * NQ, len(jobs)):
                emit_evac(v, jn)

        def tail_gpsimd(g):
            g.wait_ge(pe_q.h, pe_q.v(NQ))
            wdma(g, wv, W_ab[0])

        emit_ln(f"a{li}", xsrc, TT, 0, xlnT, T, psT,
                ext={"tile_tensor": tile_tensor, "tail_tensor": tail_tensor,
                     "tile_scalar": tile_scalar, "tail_scalar": tail_scalar,
                     "tail_vector": tail_vector, "tail_gpsimd": tail_gpsimd})

        pe_q.advance(len(jobs))
        s_qe.advance(2 * NQ)
        v_qe.advance(TT * 2)
        sw.advance(48)
        wcnt["n"] += 1
        if isb:
            pe_bc.advance(2)
        psc.close()
        subR.close()

        e_sb = sub.enter_context(nc.sbuf_tensor(f"e{li}", [128, T], BF16))
        eT = sub.enter_context(nc.sbuf_tensor(f"eT{li}", [128, T], BF16))
        au = (sub.enter_context(nc.sbuf_tensor(f"au{li}", [128, TT * D], BF16))
              if isb else None)

        pe_s, v_max, s_exp, v_misc, s_et, pe_a, v_res, s_au, pe_t = (
            S["pe_s"], S["v_max"], S["s_exp"], S["v_misc"], S["s_et"],
            S["pe_a"], S["v_res"], S["s_au"], S["pe_t"])
        sx2 = [S["sx2_0"], S["sx2_1"]]
        sst = [S["st0"], S["st1"]]

        def n_sb(i):
            return 4 if not isb else (((i + 1) * 128 + 511) // 512)

        def n_tr(i):
            return TT if not isb else (i + 1)

        psc = ExitStack()
        Sps = psc.enter_context(nc.psum_tensor(f"Sps{li}", [128, T], F32))
        att = psc.enter_context(nc.psum_tensor(f"attp{li}", [128, D], F32))
        psT2 = [psc.enter_context(nc.psum_tensor(f"psTt{li}{i}", [128, 128], BF16))
                for i in range(2)]

        def emit_scores(t, i):
            for sb in range(n_sb(i)):
                for dt in range(HC):
                    ins = t.matmul(Sps[:, sb * 512:(sb + 1) * 512],
                                   qT[:, dt * T + i * 128:dt * T + (i + 1) * 128],
                                   kT[:, dt * T + sb * 512:dt * T + (sb + 1) * 512],
                                   start=(dt == 0), stop=(dt == HC - 1))
            return ins

        if isb:

            pgm = att[0:1, 0:128]
            pbc1 = att[:, 512:513]
            with nc.Block(f"bp2{li}") as blk:
                def emit_attb(t, i, tr0):
                    if i >= 1:
                        t.wait_ge(s_au.h, s_au.v(i))
                    for sc in range(n_tr(i)):
                        t.wait_ge(s_et.h, s_et.v(tr0 + sc + 1))
                        for dc in range(2):
                            ins = t.matmul(att[:, dc * 512:(dc + 1) * 512],
                                           eT[:, sc * 128:(sc + 1) * 128],
                                           v_sb[:, sc * D + dc * 512:sc * D + (dc + 1) * 512],
                                           start=(sc == 0), stop=(sc == n_tr(i) - 1))
                    pe_a.inc(ins)

                @blk.tensor
                def _(t):
                    trc = 0
                    for i in range(TT):
                        if i >= 1:
                            t.wait_ge(s_exp.h, s_exp.v(i))
                        pe_s.inc(emit_scores(t, i))
                        if i >= 1:
                            emit_attb(t, i - 1, trc - n_tr(i - 1))
                        t.wait_ge(s_exp.h, s_exp.v(i + 1))
                        for sc in range(n_tr(i)):
                            if trc >= 2:
                                t.wait_ge(s_et.h, s_et.v(trc - 1))
                            pe_t.inc(t.transpose(psT2[trc % 2][:, :],
                                                 e_sb[:, sc * 128:(sc + 1) * 128],
                                                 ids_b[:, :]))
                            trc += 1
                    emit_attb(t, TT - 1, trc - n_tr(TT - 1))

                @blk.vector
                def _(v):
                    for i in range(TT):
                        b = i % 2
                        v.wait_ge(pe_s.h, pe_s.v(i + 1))
                        vch.cut(v.tensor_add(Sps[:, i * 128:(i + 1) * 128],
                                             Sps[:, i * 128:(i + 1) * 128],
                                             cam_s[:, :]))
                        vch.wait(v)
                        vch.cut(v.reduce_max(stmx[:, i:i + 1],
                                             Sps[:, 0:(i + 1) * 128], axis=AX.X))
                        vch.wait(v)
                        if i >= 2:
                            v.wait_ge(s_exp.h, s_exp.v(i - 1))
                        v_max.inc(v.tensor_scalar_mul(negm[:, b:b + 1],
                                                      stmx[:, i:i + 1], -1.0))

                @blk.scalar
                def _(s):
                    trc = 0
                    for i in range(TT):
                        b = i % 2
                        w = (i + 1) * 128
                        s.wait_ge(v_max.h, v_max.v(i + 1))
                        s_exp.inc(s.activation(e_sb[:, 0:w], Sps[:, 0:w], AF.Exp,
                                               bias=negm[:, b:b + 1],
                                               accum_out=strs[:, i:i + 1]))
                        for sc in range(n_tr(i)):
                            s.wait_ge(pe_t.h, pe_t.v(trc + 1))
                            s_et.inc(s.copy(eT[:, sc * 128:(sc + 1) * 128],
                                            psT2[trc % 2][:, :]))
                            trc += 1
                        if i >= 1:
                            s.wait_ge(pe_a.h, pe_a.v(i))
                            s_au.inc(s.copy(au[:, (i - 1) * D:i * D], att[:, :]))
                    s.wait_ge(pe_a.h, pe_a.v(TT))
                    s_au.inc(s.copy(au[:, (TT - 1) * D:TT * D], att[:, :]))

            ntr_tot = sum(n_tr(i) for i in range(TT))
            pe_s.advance(TT)
            v_max.advance(TT)
            s_exp.advance(TT)
            pe_t.advance(ntr_tot)
            s_et.advance(ntr_tot)
            pe_a.advance(TT)
            s_au.advance(TT)

            with nc.Block(f"gm1{li}") as blk:
                @blk.vector
                def _(v):
                    v.reduce_max(gmx[:, :], stmx[:, :], axis=AX.X)
            with nc.Block(f"gm2{li}") as blk:
                @blk.tensor
                def _(t):
                    t.transpose(pgm, gmx[:, :], ids_f[:, :])
            with nc.Block(f"gm3{li}") as blk:
                @blk.vector
                def _(v):
                    v.reduce_max(m11[:, :], pgm, axis=AX.X)
            with nc.Block(f"gm4{li}") as blk:
                @blk.tensor
                def _(t):
                    t.matmul(pbc1, ones_s[0:1, :], m11[0:1, :],
                             start=True, stop=True)
            with nc.Block(f"gm5{li}") as blk:
                @blk.vector
                def _(v):
                    v.tensor_scalar_mul(negM[:, :], pbc1, -1.0)
            with nc.Block(f"sm0{li}") as blk:
                @blk.scalar
                def _(s):
                    s.activation(fts[:, :], stmx[:, :], AF.Exp, bias=negM[:, 0:1])
            with nc.Block(f"sm1{li}") as blk:
                @blk.vector
                def _(v):
                    vch.cut(v.tensor_mul(gw8[:, :], fts[:, :], strs[:, :]))
                    vch.wait(v)
                    v.reduce_sum(gsm[:, :], gw8[:, :], axis=AX.X)
            with nc.Block(f"sm2{li}") as blk:
                @blk.tensor
                def _(t):
                    t.transpose(pgm, gsm[:, :], ids_f[:, :])
            with nc.Block(f"sm3{li}") as blk:
                @blk.vector
                def _(v):
                    v.reduce_sum(m11[:, :], pgm, axis=AX.X)
            with nc.Block(f"sm4{li}") as blk:
                @blk.tensor
                def _(t):
                    t.matmul(pbc1, ones_s[0:1, :], m11[0:1, :],
                             start=True, stop=True)
            with nc.Block(f"sm5{li}") as blk:
                @blk.vector
                def _(v):
                    vch.cut(v.reciprocal(sinv[:, :], pbc1))
                    vch.wait(v)
                    v.tensor_scalar(fts[:, :], fts[:, :], sinv[:, 0:1], None,
                                    ALU.mult)

            with nc.Block(f"bres{li}") as blk:
                @blk.sync
                def _(sy):
                    for i in range(TT + 2):
                        if i >= 2:
                            sy.wait_ge(v_res.h, v_res.v(i - 1))
                            b2 = (i - 2) % 2
                            sst[b2].inc(sy.dma_start(
                                xdst.ap()[(i - 2) * 128:(i - 1) * 128, :],
                                xnew[:, b2 * D:b2 * D + D]), 16)
                        if i < TT:
                            b = i % 2
                            sx2[b].inc(sy.dma_start(
                                xt2[:, b * D:b * D + D],
                                xsrc.ap()[i * 128:(i + 1) * 128, :]), 16)

                @blk.vector
                def _(v):
                    for i in range(TT):
                        b = i % 2
                        v.wait_ge(sx2[b].h, sx2[b].v(16 * (i // 2 + 1)))
                        if i >= 2:
                            v.wait_ge(sst[b].h, sst[b].v(16 * (i // 2)))
                        v_res.inc(v.scalar_tensor_tensor(
                            xnew[:, b * D:b * D + D], au[:, i * D:(i + 1) * D],
                            fts[:, i:i + 1], xt2[:, b * D:b * D + D],
                            ALU.mult, ALU.add))
            sx2[0].advance(16 * (TT // 2))
            sx2[1].advance(16 * (TT // 2))
            sst[0].advance(16 * (TT // 2))
            sst[1].advance(16 * (TT // 2))
            v_res.advance(TT)

        else:
            with nc.Block(f"acore{li}") as blk:
                def emit_att(t, i, tr0):
                    if i >= 1:
                        t.wait_ge(v_res.h, v_res.v(i))
                    for sc in range(TT):
                        t.wait_ge(s_et.h, s_et.v(tr0 + sc + 1))
                        for dc in range(2):
                            ins = t.matmul(att[:, dc * 512:(dc + 1) * 512],
                                           eT[:, sc * 128:(sc + 1) * 128],
                                           v_sb[:, sc * D + dc * 512:sc * D + (dc + 1) * 512],
                                           start=(sc == 0), stop=(sc == TT - 1))
                    pe_a.inc(ins)

                @blk.tensor
                def _(t):
                    trc = 0
                    for i in range(TT):
                        if i >= 1:
                            t.wait_ge(s_exp.h, s_exp.v(i))
                        pe_s.inc(emit_scores(t, i))
                        if i >= 1:
                            emit_att(t, i - 1, trc - TT)
                        t.wait_ge(s_exp.h, s_exp.v(i + 1))
                        for sc in range(TT):
                            if trc >= 2:
                                t.wait_ge(s_et.h, s_et.v(trc - 1))
                            pe_t.inc(t.transpose(psT2[trc % 2][:, :],
                                                 e_sb[:, sc * 128:(sc + 1) * 128],
                                                 ids_b[:, :]))
                            trc += 1
                    emit_att(t, TT - 1, trc - TT)

                def emit_stt(v, i):
                    b = i % 2
                    v.wait_ge(pe_a.h, pe_a.v(i + 1))
                    v.wait_ge(sx2[b].h, sx2[b].v(16 * (i // 2 + 1)))
                    if i >= 2:
                        v.wait_ge(sst[b].h, sst[b].v(16 * (i // 2)))
                    v.wait_ge(v_misc.h, v_misc.v(i + 1))
                    v_res.inc(v.scalar_tensor_tensor(
                        xnew[:, b * D:b * D + D], att[:, :],
                        winv[:, b:b + 1], xt2[:, b * D:b * D + D],
                        ALU.mult, ALU.add))

                @blk.vector
                def _(v):
                    for i in range(TT):
                        b = i % 2
                        v.wait_ge(pe_s.h, pe_s.v(i + 1))
                        v_max.inc(v.reduce_max(negm[:, b:b + 1], Sps[:, :],
                                               axis=AX.X, negate=True))
                        v.wait_ge(s_exp.h, s_exp.v(i + 1))
                        v_misc.inc(v.reciprocal(winv[:, b:b + 1],
                                                rsum[:, b:b + 1]))
                        if i >= 1:
                            emit_stt(v, i - 1)
                    emit_stt(v, TT - 1)

                @blk.scalar
                def _(s):
                    trc = 0
                    for i in range(TT):
                        b = i % 2
                        s.wait_ge(v_max.h, v_max.v(i + 1))
                        s_exp.inc(s.activation(e_sb[:, :], Sps[:, :], AF.Exp,
                                               bias=negm[:, b:b + 1],
                                               accum_out=rsum[:, b:b + 1]))
                        for sc in range(TT):
                            s.wait_ge(pe_t.h, pe_t.v(trc + 1))
                            s_et.inc(s.copy(eT[:, sc * 128:(sc + 1) * 128],
                                            psT2[trc % 2][:, :]))
                            trc += 1

                @blk.sync
                def _(sy):
                    for i in range(TT + 2):
                        if i >= 2:
                            sy.wait_ge(v_res.h, v_res.v(i - 1))
                            b2 = (i - 2) % 2
                            sst[b2].inc(sy.dma_start(
                                xdst.ap()[(i - 2) * 128:(i - 1) * 128, :],
                                xnew[:, b2 * D:b2 * D + D]), 16)
                        if i < TT:
                            b = i % 2
                            sx2[b].inc(sy.dma_start(
                                xt2[:, b * D:b * D + D],
                                xsrc.ap()[i * 128:(i + 1) * 128, :]), 16)

            pe_s.advance(TT)
            v_max.advance(TT)
            s_exp.advance(TT)
            pe_t.advance(TT * TT)
            s_et.advance(TT * TT)
            pe_a.advance(TT)
            v_misc.advance(TT)
            v_res.advance(TT)
            sx2[0].advance(16 * (TT // 2))
            sx2[1].advance(16 * (TT // 2))
            sst[0].advance(16 * (TT // 2))
            sst[1].advance(16 * (TT // 2))

        psc.close()
        sub.close()

    def emit_ffn(li, xsrc, xdst):
        sub = ExitStack()
        Wg_sb = [sub.enter_context(nc.sbuf_tensor(f"Wg{li}{i}", [128, HC * FCH], BF16))
                 for i in range(2)]
        Wu_sb = [sub.enter_context(nc.sbuf_tensor(f"Wu{li}{i}", [128, HC * FCH], BF16))
                 for i in range(2)]
        Wd_sb = [sub.enter_context(nc.sbuf_tensor(f"Wd{li}{i}", [128, HC * D], BF16))
                 for i in range(2)]
        xlnT = sub.enter_context(nc.sbuf_tensor(f"xlnTf{li}", [128, HC * TH], BF16))
        acc = sub.enter_context(nc.sbuf_tensor(f"acc{li}", [128, HC * TH], F32))
        hact = [sub.enter_context(nc.sbuf_tensor(f"hact{li}{i}", [128, HC * 512], BF16))
                for i in range(2)]
        cbB = [sub.enter_context(nc.sbuf_tensor(f"cbB{li}{i}", [128, TH], BF16))
               for i in range(2)]
        gg = sub.enter_context(nc.sbuf_tensor(f"gg{li}", [128, 2 * 512], BF16))

        swf = [S["swf0"], S["swf1"]]

        def issue_chunk(g, c, cb):
            e, fc = divmod(c, NFC)
            sw = swf[c % 2]
            sw.inc(g.dma_start(Wg_sb[cb][:, :], gch.ap()[e][fc]), 16)
            sw.inc(g.dma_start(Wu_sb[cb][:, :], uch.ap()[e][fc]), 16)
            sw.inc(g.dma_start(Wd_sb[cb][:, :], dch.ap()[e][fc]), 16)

        for half in range(2):
            goff = half * TH
            hl = f"{li}h{half}"

            with nc.Block(f"fw{hl}") as blk:
                @blk.gpsimd
                def _(g):
                    issue_chunk(g, 0, 0)
                    issue_chunk(g, 1, 1)

            psc = ExitStack()
            psT = [psc.enter_context(nc.psum_tensor(f"psTf{hl}{i}", [128, 128], BF16))
                   for i in range(2)]
            pr = [psc.enter_context(nc.psum_tensor(f"pr{hl}{i}", [128, 8], F32))
                  for i in range(2)]
            pT8 = psc.enter_context(nc.psum_tensor(f"pT8{hl}", [8, 8 * 128], BF16))
            pe_r, v_rm, s_rexp, v_cmb, s_cb, pe_t8, v_t8 = (
                S["pe_r"], S["v_rm"], S["s_rexp"], S["v_cmb"], S["s_cb"],
                S["pe_t8"], S["v_t8"])

            def r_tensor(t, i):
                if i >= 2:
                    t.wait_ge(v_rm.h, v_rm.v(i - 1))
                t.wait_ge(S["v_te"].h, S["v_te"].v((i + 1) * 4))
                t.wait_ge(S["s_te"].h, S["s_te"].v((i + 1) * 4))
                for hc in range(HC):
                    ins = t.matmul(pr[i % 2][:, :],
                                   xlnT[:, hc * TH + i * 128:hc * TH + (i + 1) * 128],
                                   gw_s[:, hc * E:(hc + 1) * E],
                                   start=(hc == 0), stop=(hc == HC - 1))
                pe_r.inc(ins)
                if i >= 1:
                    t.wait_ge(s_cb.h, s_cb.v(i))
                    pe_t8.inc(t.transpose(pT8[0:8, (i - 1) * 128:i * 128],
                                          rc16[:, ((i - 1) % 2) * 8:((i - 1) % 2) * 8 + 8], ids_b[:, :]))

            def rt_tensor(t):
                t.wait_ge(s_cb.h, s_cb.v(THT))
                pe_t8.inc(t.transpose(pT8[0:8, (THT - 1) * 128:THT * 128],
                                      rc16[:, ((THT - 1) % 2) * 8:((THT - 1) % 2) * 8 + 8], ids_b[:, :]))

            def r_partA(v, i):
                b = i % 2
                q1 = rq1[:, b * 8:(b + 1) * 8]
                q2 = rq2[:, b * 8:(b + 1) * 8]
                v.wait_ge(pe_r.h, pe_r.v(i + 1))
                if i >= 2:
                    v.wait_ge(s_rexp.h, s_rexp.v(i - 1))
                vch.cut(v.reduce_max(rm1[:, b:b + 1], pr[i % 2][:, :], axis=AX.X))
                vch.wait(v)
                vch.cut(v.tensor_scalar(q1, pr[i % 2][:, :], rm1[:, b:b + 1],
                                        None, ALU.is_equal))
                vch.wait(v)
                vch.cut(v.tensor_scalar_mul(q2, q1, 1.0e30))
                vch.wait(v)
                vch.cut(v.tensor_sub(q2, pr[i % 2][:, :], q2))
                vch.wait(v)
                vch.cut(v.reduce_max(rm2[:, b:b + 1], q2, axis=AX.X))
                vch.wait(v)
                vch.cut(v.tensor_scalar(q2, q2, rm2[:, b:b + 1], None,
                                        ALU.is_equal))
                v_rm.inc(v.tensor_sub(rden[:, b:b + 1], rm1[:, b:b + 1],
                                      rm2[:, b:b + 1]))

            def r_partB(v, i):
                b = i % 2
                v.wait_ge(s_rexp.h, s_rexp.v(i + 1))
                if i >= 1:
                    v.wait_ge(s_cb.h, s_cb.v(i))
                v.tensor_scalar(rw2[:, b:b + 1], rw1[:, b:b + 1], -1.0, 1.0,
                                ALU.mult, ALU.add)
                vch.cut(v.tensor_scalar(rcmb[:, :], rq1[:, b * 8:(b + 1) * 8],
                                        rw1[:, b:b + 1], None, ALU.mult))
                vch.wait(v)
                v_cmb.inc(v.scalar_tensor_tensor(
                    rcmb[:, :], rq2[:, b * 8:(b + 1) * 8], rw2[:, b:b + 1],
                    rcmb[:, :], ALU.mult, ALU.add))

            def r_vector(v, i):
                if i >= 1:
                    r_partB(v, i - 1)
                r_partA(v, i)

            def rt_vector(v):
                r_partB(v, THT - 1)
                v.wait_ge(pe_t8.h, pe_t8.v(THT))
                v_t8.inc(v.tensor_copy(combT8[0:8, :], pT8[0:8, :]))

            def r_scalar(s, i):
                if i >= 1:
                    s.wait_ge(v_cmb.h, v_cmb.v(i))
                    s_cb.inc(s.copy(rc16[:, ((i - 1) % 2) * 8:((i - 1) % 2) * 8 + 8], rcmb[:, :]))
                s.wait_ge(v_rm.h, v_rm.v(i + 1))
                s_rexp.inc(s.activation(rw1[:, i % 2:i % 2 + 1],
                                        rden[:, i % 2:i % 2 + 1], AF.Sigmoid))

            def rt_scalar(s):
                s.wait_ge(v_cmb.h, v_cmb.v(THT))
                s_cb.inc(s.copy(rc16[:, ((THT - 1) % 2) * 8:((THT - 1) % 2) * 8 + 8], rcmb[:, :]))

            emit_ln(f"f{hl}", xsrc, THT, goff, xlnT, TH, psT,
                    ext={"tile_tensor": r_tensor, "tail_tensor": rt_tensor,
                         "tile_vector": r_vector, "tail_vector": rt_vector,
                         "tile_scalar": r_scalar, "tail_scalar": rt_scalar})

            pe_r.advance(THT)
            v_rm.advance(THT)
            s_rexp.advance(THT)
            v_cmb.advance(THT)
            s_cb.advance(THT)
            pe_t8.advance(THT)
            v_t8.advance(1)
            psc.close()

            psc = ExitStack()
            pg = [psc.enter_context(nc.psum_tensor(f"pg{hl}{i}", [128, 512], F32))
                  for i in range(2)]
            pu = [psc.enter_context(nc.psum_tensor(f"pu{hl}{i}", [128, 512], F32))
                  for i in range(2)]
            pd = [psc.enter_context(nc.psum_tensor(f"pd{hl}{i}", [128, 512], F32))
                  for i in range(2)]
            pcb = psc.enter_context(nc.psum_tensor(f"pcb{hl}", [128, TH], F32))

            pe_cb, v_cb, pe_g, pe_u, s_act, v_hact, pe_d, v_ev = (
                S["pe_cb"], S["v_cb"], S["pe_g"], S["pe_u"], S["s_act"],
                S["v_hact"], S["pe_d"], S["v_ev"])

            with nc.Block(f"f2{hl}") as blk:
                @blk.tensor
                def _(t):
                    for c in range(NCH):
                        e, fc = divmod(c, NFC)
                        if fc == 0:
                            if e >= 1:
                                t.wait_ge(v_cb.h, v_cb.v(e))
                            for tc in range(2):
                                ins = t.matmul(pcb[:, tc * 512:(tc + 1) * 512],
                                               sel_s[0:8, e * 128:(e + 1) * 128],
                                               combT8[0:8, tc * 512:(tc + 1) * 512],
                                               start=True, stop=True)
                            pe_cb.inc(ins)
                        t.wait_ge(swf[c % 2].h, swf[c % 2].v(48 * (c // 2 + 1)))
                        for tb in range(2):
                            for j in range(HC):
                                gj = (c * 2 + tb) * HC + j
                                if gj >= 2:
                                    t.wait_ge(s_act.h, s_act.v(gj - 1))
                                for hc in range(HC):
                                    ins = t.matmul(pg[j % 2][:, :],
                                                   Wg_sb[c % 2][:, hc * FCH + j * 128:hc * FCH + (j + 1) * 128],
                                                   xlnT[:, hc * TH + tb * 512:hc * TH + (tb + 1) * 512],
                                                   start=(hc == 0), stop=(hc == HC - 1))
                                pe_g.inc(ins)
                                if gj >= 2:
                                    t.wait_ge(S["v_h1"].h, S["v_h1"].v(gj - 1))
                                for hc in range(HC):
                                    ins = t.matmul(pu[j % 2][:, :],
                                                   Wu_sb[c % 2][:, hc * FCH + j * 128:hc * FCH + (j + 1) * 128],
                                                   xlnT[:, hc * TH + tb * 512:hc * TH + (tb + 1) * 512],
                                                   start=(hc == 0), stop=(hc == HC - 1))
                                pe_u.inc(ins)
                            for h2 in range(HC):
                                gdh = (c * 2 + tb) * HC + h2
                                if gdh >= 2:
                                    t.wait_ge(v_ev.h, v_ev.v(gdh - 1))
                                for j in range(HC):
                                    if h2 == 0:
                                        t.wait_ge(v_hact.h,
                                                  v_hact.v((c * 2 + tb) * HC + j + 1))
                                    ins = t.matmul(pd[h2 % 2][:, :],
                                                   Wd_sb[c % 2][:, j * D + h2 * 128:j * D + (h2 + 1) * 128],
                                                   hact[tb][:, j * 512:(j + 1) * 512],
                                                   start=(j == 0), stop=(j == HC - 1))
                                pe_d.inc(ins)

                @blk.scalar
                def _(s):
                    for c in range(NCH):
                        for tb in range(2):
                            for j in range(HC):
                                gj = (c * 2 + tb) * HC + j
                                s.wait_ge(pe_g.h, pe_g.v(gj + 1))
                                if gj >= 2:
                                    s.wait_ge(S["v_h1"].h, S["v_h1"].v(gj - 1))
                                s_act.inc(s.activation(
                                    gg[:, (j % 2) * 512:(j % 2) * 512 + 512],
                                    pg[j % 2][:, :], AF.Gelu))

                @blk.vector
                def _(v):
                    for c in range(NCH):
                        e, fc = divmod(c, NFC)
                        if fc == 0:
                            v.wait_ge(pe_cb.h, pe_cb.v(e + 1))
                            v_cb.inc(v.tensor_copy(cbB[e % 2][:, :], pcb[:, :]))
                        for tb in range(2):
                            for j in range(HC):
                                gj = (c * 2 + tb) * HC + j
                                v.wait_ge(s_act.h, s_act.v(gj + 1))
                                v.wait_ge(pe_u.h, pe_u.v(gj + 1))
                                if c >= 1:
                                    v.wait_ge(pe_d.h,
                                              pe_d.v(((c - 1) * 2 + tb + 1) * HC))
                                S["v_h1"].inc(v.tensor_mul(
                                    hact[tb][:, j * 512:(j + 1) * 512],
                                    pu[j % 2][:, :],
                                    gg[:, (j % 2) * 512:(j % 2) * 512 + 512]))
                                v.wait_ge(S["v_h1"].h, S["v_h1"].v(gj + 1))
                                v_hact.inc(v.tensor_mul(
                                    hact[tb][:, j * 512:(j + 1) * 512],
                                    hact[tb][:, j * 512:(j + 1) * 512],
                                    cbB[e % 2][:, tb * 512:(tb + 1) * 512]))
                            for h2 in range(HC):
                                gdh = (c * 2 + tb) * HC + h2
                                v.wait_ge(pe_d.h, pe_d.v(gdh + 1))
                                dst = acc[:, h2 * TH + tb * 512:h2 * TH + (tb + 1) * 512]
                                if c == 0:
                                    v_ev.inc(v.tensor_copy(dst, pd[h2 % 2][:, :]))
                                else:
                                    v_ev.inc(v.tensor_tensor(dst, pd[h2 % 2][:, :],
                                                             dst, ALU.add))

                @blk.gpsimd
                def _(g):
                    for c in range(2, NCH):
                        g.wait_ge(pe_d.h, pe_d.v((c - 1) * 2 * HC))
                        issue_chunk(g, c, c % 2)

            pe_cb.advance(E)
            v_cb.advance(E)
            pe_g.advance(NCH * 2 * HC)
            pe_u.advance(NCH * 2 * HC)
            s_act.advance(NCH * 2 * HC)
            S["v_h1"].advance(NCH * 2 * HC)
            v_hact.advance(NCH * 2 * HC)
            pe_d.advance(NCH * 2 * HC)
            v_ev.advance(NCH * 2 * HC)
            swf[0].advance(48 * (NCH // 2))
            swf[1].advance(48 * (NCH // 2))
            psc.close()

            psc = ExitStack()
            pF3 = [psc.enter_context(nc.psum_tensor(f"pF3{hl}{i}", [128, 128], F32))
                   for i in range(2)]
            pe_t, v_f3 = S["pe_t"], S["v_f3"]
            sx2 = [S["sx2_0"], S["sx2_1"]]
            sst = [S["st0"], S["st1"]]
            with nc.Block(f"f3{hl}") as blk:
                @blk.tensor
                def _(t):
                    trc = 0
                    for i in range(THT):
                        for j in range(HC):
                            if trc >= 2:
                                t.wait_ge(v_f3.h, v_f3.v(trc - 1))
                            pe_t.inc(t.transpose(
                                pF3[trc % 2][:, :],
                                acc[:, j * TH + i * 128:j * TH + (i + 1) * 128],
                                ids_f[:, :]))
                            trc += 1

                @blk.vector
                def _(v):
                    trc = 0
                    for i in range(THT):
                        b = i % 2
                        v.wait_ge(sx2[b].h, sx2[b].v(16 * (i // 2 + 1)))
                        if i >= 2:
                            v.wait_ge(sst[b].h, sst[b].v(16 * (i // 2)))
                        for j in range(HC):
                            v.wait_ge(pe_t.h, pe_t.v(trc + 1))
                            v_f3.inc(v.tensor_tensor(
                                xnew[:, b * D + j * 128:b * D + (j + 1) * 128],
                                pF3[trc % 2][:, :],
                                xt2[:, b * D + j * 128:b * D + (j + 1) * 128],
                                ALU.add))
                            trc += 1

                @blk.sync
                def _(sy):
                    for i in range(THT + 2):
                        if i >= 2:
                            sy.wait_ge(v_f3.h, v_f3.v(HC * (i - 1)))
                            b2 = (i - 2) % 2
                            sst[b2].inc(sy.dma_start(
                                xdst.ap()[goff + (i - 2) * 128:goff + (i - 1) * 128, :],
                                xnew[:, b2 * D:b2 * D + D]), 16)
                        if i < THT:
                            b = i % 2
                            sx2[b].inc(sy.dma_start(
                                xt2[:, b * D:b * D + D],
                                xsrc.ap()[goff + i * 128:goff + (i + 1) * 128, :]), 16)

            pe_t.advance(THT * HC)
            v_f3.advance(THT * HC)
            sx2[0].advance(16 * ((THT + 1) // 2))
            sx2[1].advance(16 * (THT // 2))
            sst[0].advance(16 * ((THT + 1) // 2))
            sst[1].advance(16 * (THT // 2))
            psc.close()

        sub.close()

    chain = [
        ("a", xin, out if n_sub == 1 else sp0),
        ("f", sp0, out if n_sub == 2 else sp1),
        ("b", sp1, out if n_sub == 3 else sp0),
        ("f", sp0, out),
    ][:n_sub]

    for li, (kind, src, dst) in enumerate(chain):
        if kind == "f":
            emit_ffn(li, src, dst)
        else:
            emit_attn(kind, li, src, dst)

    with nc.Block("fin") as blk:
        @blk.sync
        def _(sy):
            sy.wait_ge(S["st0"].h, S["st0"].v(0))
            sy.wait_ge(S["st1"].h, S["st1"].v(0))

    st.close()
    return nc


def _prep_inputs(kw):
    f32 = np.float32
    x = np.asarray(kw["L0_attn_a_x"], dtype=f32)
    c = np.ascontiguousarray
    def attn_w(w_hd):
        return c(w_hd.reshape(HC, 128, D).transpose(1, 0, 2).reshape(128, HC * D))

    gup = np.asarray(kw["L0_ffn_gate_up_w"], f32)
    dw = np.asarray(kw["L0_ffn_down_w"], f32)
    def chunked_gu(part):
        t = part.reshape(E, NFC, FCH, HC, 128)
        return c(t.transpose(0, 1, 4, 3, 2).reshape(E, NFC, 128, HC * FCH))
    dt_ = dw.reshape(E, D, NFC, HC, 128)
    dch_np = c(dt_.transpose(0, 2, 4, 3, 1).reshape(E, NFC, 128, HC * D))
    base = {
        "aWqT": attn_w(np.asarray(kw["L0_attn_a_Wq"], f32).T),
        "aWkT": attn_w(np.asarray(kw["L0_attn_a_Wk"], f32).T),
        "aWvT": attn_w(np.asarray(kw["L0_attn_a_Wv"], f32).T),
        "bWq": attn_w(np.asarray(kw["L0_attn_b_Wq"], f32)),
        "bWk": attn_w(np.asarray(kw["L0_attn_b_Wk"], f32)),
        "bWv": attn_w(np.asarray(kw["L0_attn_b_Wv"], f32)),
        "bqt": c(np.asarray(kw["L0_attn_b_bq"], f32).reshape(HC, 128).T),
        "bkt": c(np.asarray(kw["L0_attn_b_bk"], f32).reshape(HC, 128).T),
        "bvr": c(np.asarray(kw["L0_attn_b_bv"], f32).reshape(1, D)),
        "gwT": c(np.asarray(kw["L0_ffn_gate_w"], f32).T),
        "gch": chunked_gu(gup[:, :FF, :]),
        "uch": chunked_gu(gup[:, FF:, :]),
        "dch": dch_np,
    }
    in_maps = []
    for core in range(8):
        m = dict(base)
        m["xin"] = c(x[core])
        in_maps.append(m)
    return in_maps


def _run(kw, n_sub=4):
    in_maps = _prep_inputs(kw)
    nc = build_program(n_sub)
    res = run_bass_kernel_spmd(nc, in_maps, core_ids=list(range(8)))
    return np.stack([r["out"] for r in res.results], axis=0)


def kernel(**kw):
    return _run(kw, 4)
